# version 21
# baseline (speedup 1.0000x reference)
import numpy as np
import concourse.bass as bass
import concourse.mybir as mybir
from concourse.bass_utils import run_bass_kernel_spmd
from concourse.alu_op_type import AluOpType
from contextlib import ExitStack

F32 = mybir.dt.float32
BF16 = mybir.dt.bfloat16
AF = mybir.ActivationFunctionType
AX = mybir.AxisListType

D = 1024
S = 8192
NPAIR = 8
NC = 8
TOK = 2048
NSLOT = 4
QT = 512
NKT = 64
H = 4096
HC = 32
ER = 24
EPS = 1e-5
SMSCALE = 0.125
LSCALE = 0.125
SET_COLS = 1536

DEBUG = False
FLASH_PAIRS = NPAIR
FLASH_SLOTS = NSLOT
FLASH_SETS = 10**9
FLASH_EVAC = True
FLASH_KT_LIMIT = 10**9


class SemT:
    def __init__(self, h):
        self.h = h
        self.n = 0

    def inc(self, instr, k=1):
        instr.then_inc(self.h, k)
        self.n += k
        return instr


class _Dummy:
    def then_inc(self, *a, **k):
        return self


class _Proxy:
    def __getattr__(self, name):
        return lambda *a, **k: _Dummy()


_PROXY = _Proxy()


class Prog:
    ENGS = ("tensor", "vector", "scalar", "gpsimd", "sync")

    def __init__(self):
        self.ops = {e: [] for e in self.ENGS}
        self.last_wait = {}

    def emit(self, eng, fn):
        fn(_PROXY)
        self.ops[eng].append(fn)

    def wait(self, eng, sem, val=None):
        v = sem.n if val is None else val
        key = (eng, id(sem))
        if v > 0 and v > self.last_wait.get(key, 0):
            self.last_wait[key] = v
            self.ops[eng].append(lambda E, s=sem.h, v=v: E.wait_ge(s, v))


class _StopBuild(Exception):
    pass


def build_nc(debug=False, phases=99):
    try:
        return _build_nc_inner(debug, phases)
    except _StopBuild as e:
        nc, P = e.args
        _emit_block(nc, P)
        return nc


def _build_nc_inner(debug=False, phases=99):
    nc = bass.Bass("TRN2", target_bir_lowering=False, debug=False)
    P = Prog()

    def ckpt(n):
        if phases < n:
            raise _StopBuild(nc, P)

    xTb = nc.dram_tensor("xTb", [D, S], F32, kind="ExternalInput").ap()
    xTo = nc.dram_tensor("xTo", [D, TOK], F32, kind="ExternalInput").ap()
    wqT = nc.dram_tensor("wqT", [D, D], F32, kind="ExternalInput").ap()
    wkT = nc.dram_tensor("wkT", [D, D], F32, kind="ExternalInput").ap()
    wvT = nc.dram_tensor("wvT", [D, D], F32, kind="ExternalInput").ap()
    woT = nc.dram_tensor("woT", [D, D], F32, kind="ExternalInput").ap()
    boc = nc.dram_tensor("boc", [128, NC], F32, kind="ExternalInput").ap()
    w1d = nc.dram_tensor("w1", [D, H], F32, kind="ExternalInput").ap()
    b1c = nc.dram_tensor("b1c", [128, HC], F32, kind="ExternalInput").ap()
    w2d = nc.dram_tensor("w2", [H, D], F32, kind="ExternalInput").ap()
    b2c = nc.dram_tensor("b2c", [128, NC], F32, kind="ExternalInput").ap()
    a1r1 = nc.dram_tensor("a1r1", [D, 32], F32, kind="ExternalInput").ap()
    b1cat = nc.dram_tensor("b1cat", [ER, H], F32, kind="ExternalInput").ap()
    a2r2 = nc.dram_tensor("a2r2", [H, 32], F32, kind="ExternalInput").ap()
    b2cat = nc.dram_tensor("b2cat", [ER, D], F32, kind="ExternalInput").ap()
    thr = nc.dram_tensor("thr", [128, 16], F32, kind="ExternalInput").ap()
    yout = nc.dram_tensor("y", [D, TOK], F32, kind="ExternalOutput").ap()

    kscr = nc.dram_tensor("kscr", [NPAIR, 128, S], BF16, kind="Internal").ap()
    vscr = nc.dram_tensor("vscr", [NPAIR, 128, NKT * 130], BF16, kind="Internal").ap()
    lscr = nc.dram_tensor("lscr", [2, 32, QT], F32, kind="Internal").ap()
    ctxscr = nc.dram_tensor("ctxscr", [NPAIR, 128, TOK], BF16, kind="Internal").ap()
    w1bf = nc.dram_tensor("w1bf", [HC, 128, 1024], BF16, kind="Internal").ap()
    w2bf = nc.dram_tensor("w2bf", [NC, 128, 4096], BF16, kind="Internal").ap()

    dbg = {}
    if debug:
        for nm, shp in [("dbg_q", [D, TOK]), ("dbg_k", [D, S]),
                        ("dbg_v", [NPAIR * 128, NKT * 130]),
                        ("dbg_ctx", [D, TOK]), ("dbg_l", [2, 32, QT]),
                        ("dbg_u1", [D, TOK]), ("dbg_y1", [D, TOK]),
                        ("dbg_h", [H, QT]), ("dbg_u2", [D, TOK])]:
            dbg[nm] = nc.dram_tensor(nm, shp, F32, kind="ExternalOutput").ap()

    root = ExitStack()
    sbR = lambda nm, shp, dt: root.enter_context(nc.sbuf_tensor(nm, shp, dt))
    mksem = lambda nm: SemT(root.enter_context(nc.semaphore(nm)))

    lrow = sbR("lrow", [33, 2 * QT], F32)
    iot = sbR("iot", [128, QT], F32)
    thrS = sbR("thrS", [128, 16], F32)
    ident = sbR("ident", [128, 128], BF16)
    sel33 = sbR("sel33", [33, 128], F32)
    ones1 = sbR("ones1", [1, 128], F32)
    ones128 = sbR("ones128", [128, 1], F32)
    epsc = sbR("epsc", [1, 1], F32)
    bocS = sbR("bocS", [128, NC], F32)
    b1cS = sbR("b1cS", [128, HC], F32)
    b2cS = sbR("b2cS", [128, NC], F32)
    a1r1S = sbR("a1r1S", [128, NC * 32], BF16)
    a2r2S = sbR("a2r2S", [128, HC * 32], BF16)
    b1S = sbR("b1S", [ER, H], BF16)
    b2S = sbR("b2S", [ER, D], BF16)
    woS = sbR("woS", [128, NC * D], BF16)

    sW = mksem("sW"); sMsk = mksem("sMsk")
    sXp = [mksem("sX0"), mksem("sX1")]
    sXF = [mksem("sXF0"), mksem("sXF1")]; sXC = mksem("sXC")
    sTile = mksem("sTile"); sPE = mksem("sPE")
    sEvKQ = mksem("sEvKQ"); sEvV = mksem("sEvV")
    sKWp = [mksem("sKW0"), mksem("sKW1")]
    sVWp = [mksem("sVW0"), mksem("sVW1")]
    sStgp = [mksem("sStg0"), mksem("sStg1")]
    sS = mksem("sS"); sExp = mksem("sExp")
    sMa = mksem("sMa"); sAV = mksem("sAV"); sCev = mksem("sCev")
    sLWp = [mksem("sLW0"), mksem("sLW1")]
    sLd = mksem("sLd"); sLdW = mksem("sLdW"); sRe = mksem("sRe"); sNrm = mksem("sNrm")
    sLdp = [mksem("sLda"), mksem("sLdb")]
    sNrmM = mksem("sNrmM"); sNrmC = mksem("sNrmC"); sNrmX = mksem("sNrmX")
    sWoPE = mksem("sWoPE"); sWoEv = mksem("sWoEv"); sU1 = mksem("sU1")
    sLnA = mksem("sLnA"); sLnP = mksem("sLnP"); sLnR = mksem("sLnR")
    sLnB = mksem("sLnB"); sApp = mksem("sApp")
    sFz = mksem("sFz"); sFg = mksem("sFg"); sFzg = mksem("sFzg")
    sFt = mksem("sFt")
    sFW1p = [mksem("sFW1a"), mksem("sFW1b")]
    sFW2p = [mksem("sFW2a"), mksem("sFW2b")]
    sFb1 = mksem("sFb1"); sFb2 = mksem("sFb2"); sFh = mksem("sFh")
    sFu = mksem("sFu"); sFu2 = mksem("sFu2")
    sCWp = [mksem("sCW0"), mksem("sCW1")]
    sCSp = [mksem("sCS0"), mksem("sCS1")]
    sXOp = [mksem("sXO0"), mksem("sXO1"), mksem("sXO2"), mksem("sXO3")]
    sOutG = mksem("sOutG")
    sWB = mksem("sWB")
    sFW1q = [mksem("sFW1q0"), mksem("sFW1q1"), mksem("sFW1q2")]
    sOut = mksem("sOut")

    eraQ = ExitStack()
    qT = eraQ.enter_context(nc.sbuf_tensor("qT", [128, NPAIR * TOK], BF16))
    masks = eraQ.enter_context(nc.sbuf_tensor("masks", [128, 16 * QT], BF16))

    def cast_load(dst_ap, src_ap):
        P.emit("gpsimd", lambda E, o=dst_ap, i=src_ap:
               sW.inc(E.dma_start(out=o, in_=i), 16))

    p1s = ExitStack()
    sb1 = lambda nm, shp, dt: p1s.enter_context(nc.sbuf_tensor(nm, shp, dt))
    ps1 = lambda nm, shp, dt: p1s.enter_context(nc.psum_tensor(nm, shp, dt))
    wq = sb1("wq", [128, NC * D], BF16)
    wk = sb1("wk", [128, NC * D], BF16)
    wv = sb1("wv", [128, NC * D], BF16)
    xbuf = sb1("xbuf", [128, 2 * NC * QT], BF16)
    kbuf = sb1("kbuf", [128, NPAIR * 2 * QT], BF16)
    xf32 = sb1("xf32", [128, 2 * 4 * QT], F32)
    vsb = sb1("vsb", [128, 2 * 4160], BF16)
    KQp = [ps1("kqp0", [128, QT], F32), ps1("kqp1", [128, QT], F32)]
    Vp = [ps1("vp0", [128, 1024], F32), ps1("vp1", [128, 1024], F32)]

    P.emit("gpsimd", lambda E: sW.inc(E.dma_start(out=thrS[:], in_=thr[:]), 16))
    P.emit("gpsimd", lambda E: sW.inc(E.dma_start(out=bocS[:], in_=boc[:]), 16))
    P.emit("gpsimd", lambda E: sW.inc(E.dma_start(out=b1cS[:], in_=b1c[:]), 16))
    P.emit("gpsimd", lambda E: sW.inc(E.dma_start(out=b2cS[:], in_=b2c[:]), 16))
    thr_ready = sW.n
    for c in range(NC):
        cast_load(wq[:, c * D:(c + 1) * D], wqT[c * 128:(c + 1) * 128, :])
    wq_ready = sW.n
    for c in range(NC):
        cast_load(wk[:, c * D:(c + 1) * D], wkT[c * 128:(c + 1) * 128, :])
        cast_load(wv[:, c * D:(c + 1) * D], wvT[c * 128:(c + 1) * 128, :])
    wkv_ready = sW.n
    for c in range(NC):
        cast_load(woS[:, c * D:(c + 1) * D], woT[c * 128:(c + 1) * 128, :])
        cast_load(a1r1S[:, c * 32:(c + 1) * 32], a1r1[c * 128:(c + 1) * 128, :])
    for c in range(HC):
        cast_load(a2r2S[:, c * 32:(c + 1) * 32], a2r2[c * 128:(c + 1) * 128, :])
    cast_load(b1S[:], b1cat[:])
    cast_load(b2S[:], b2cat[:])
    w_ready = sW.n

    P.emit("gpsimd", lambda E: sMsk.inc(E.iota(
        iot[:], pattern=[[4, QT]], base=0, channel_multiplier=-1,
        allow_small_or_imprecise_dtypes=True)))
    P.wait("vector", sW, thr_ready)
    P.wait("vector", sMsk)
    for b_ in range(16):
        P.emit("vector", lambda E, b_=b_: sMsk.inc(E.tensor_scalar(
            masks[:, b_ * QT:(b_ + 1) * QT], iot[:], thrS[:, b_:b_ + 1], None,
            AluOpType.is_ge)))
    P.wait("gpsimd", sMsk)
    P.emit("gpsimd", lambda E: sMsk.inc(E.iota(
        iot[:], pattern=[[1, QT]], base=0, channel_multiplier=-1,
        allow_small_or_imprecise_dtypes=True)))
    P.wait("vector", sMsk)
    P.emit("vector", lambda E: sMsk.inc(E.tensor_scalar(
        ident[:], iot[:, 0:128], 0.0, None, AluOpType.is_equal)))
    P.emit("vector", lambda E: E.memset(sel33[:], 0.0))
    P.emit("vector", lambda E: E.drain())
    P.emit("vector", lambda E: E.memset(sel33[0:1, 0:64], 1.0))
    P.emit("vector", lambda E: E.memset(sel33[32:33, 64:128], 1.0))
    P.emit("vector", lambda E: E.memset(ones1[:], 1.0))
    P.emit("vector", lambda E: E.memset(ones128[:], 1.0))
    P.emit("vector", lambda E: E.memset(epsc[:], EPS))
    P.emit("vector", lambda E: sMsk.inc(E.memset(vsb[:], 1.0)))
    msk_done = sMsk.n

    kq_n = [0]; v_n = [0]; tile_n = [0]; kdma = [0]; vdma = [0]
    tile_end_pe = []

    def load_xtile(src2d, i, par):
        t = tile_n[0]
        if t >= 2:
            P.wait("gpsimd", sPE, tile_end_pe[t - 2])
            P.wait("sync", sXC, 4 * (t - 1))
            P.wait("vector", sPE, tile_end_pe[t - 2])
        for c in range(4):
            P.emit("gpsimd", lambda E,
                   o=xbuf[:, (par * NC + c) * QT:(par * NC + c + 1) * QT],
                   i_=src2d[c * 128:(c + 1) * 128, i * QT:(i + 1) * QT], sx=sXp[t % 2]:
                   sx.inc(E.dma_start(out=o, in_=i_), 16))
        for k in range(4):
            c = 4 + k
            P.emit("sync", lambda E,
                   o=xf32[:, (par * 4 + k) * QT:(par * 4 + k + 1) * QT],
                   i_=src2d[c * 128:(c + 1) * 128, i * QT:(i + 1) * QT], sx=sXF[par]:
                   sx.inc(E.dma_start(out=o, in_=i_), 16))
        P.wait("vector", sXF[par], 64 * (t // 2 + 1))
        for k in range(4):
            c = 4 + k
            P.emit("vector", lambda E,
                   o=xbuf[:, (par * NC + c) * QT:(par * NC + c + 1) * QT],
                   i_=xf32[:, (par * 4 + k) * QT:(par * 4 + k + 1) * QT]:
                   sXC.inc(E.tensor_copy(o, i_)))
        tile_n[0] += 1

    def kq_proj(par, wmat, dst_fn):
        for p in range(NPAIR):
            idx = kq_n[0]
            bank = KQp[idx % 2]
            if idx >= 2:
                P.wait("tensor", sEvKQ, idx - 1)
            for c in range(NC):
                lastmm = c == NC - 1
                def mm(E, o=bank[:], l=wmat[:, c * D + p * 128: c * D + (p + 1) * 128],
                       r=xbuf[:, (par * NC + c) * QT:(par * NC + c + 1) * QT],
                       st=(c == 0), sp=lastmm):
                    return E.matmul(o, l, r, start=st, stop=sp)
                if lastmm:
                    P.emit("tensor", lambda E, f=mm: sPE.inc(f(E)))
                else:
                    P.emit("tensor", lambda E, f=mm: f(E))
            kq_n[0] += 1
            P.wait("vector", sPE)
            dst_fn(p, bank)

    for i in range(TOK // QT):
        par = tile_n[0] % 2
        load_xtile(xTo, i, par)
        P.wait("tensor", sXp[par])
        P.wait("tensor", sXC, 4 * tile_n[0])
        if i == 0:
            P.wait("tensor", sW, wq_ready)

        def qdst(p, bank, i=i):
            P.emit("vector", lambda E,
                   o=qT[:, p * TOK + i * QT: p * TOK + (i + 1) * QT],
                   i_=bank[:]: sEvKQ.inc(E.tensor_copy(o, i_)))
        kq_proj(par, wq, qdst)
        tile_end_pe.append(sPE.n)

    for i in range(S // QT):
        par = tile_n[0] % 2
        load_xtile(xTb, i, par)
        P.wait("tensor", sXp[par])
        P.wait("tensor", sXC, 4 * tile_n[0])
        if i == 0:
            P.wait("tensor", sW, wkv_ready)

        def kdst(p, bank, i=i):
            g_, tp = i // 2, i % 2
            if g_ >= 1:
                P.wait("vector", sKWp[0], 16 * (NPAIR * (g_ - 1) + p + 1))
            P.emit("vector", lambda E,
                   o=kbuf[:, p * 1024 + tp * QT: p * 1024 + tp * QT + QT],
                   i_=bank[:]: sEvKQ.inc(E.tensor_copy(o, i_)))
            kdma[0] += 1
        kq_proj(par, wk, kdst)
        if i % 2 == 1:
            P.wait("sync", sEvKQ)
            for p in range(NPAIR):
                P.emit("sync", lambda E, p=p, i=i: sKWp[0].inc(E.dma_start(
                    out=kscr[p, :, (i - 1) * QT:(i + 1) * QT],
                    in_=kbuf[:, p * 1024:(p + 1) * 1024]), 16))

        tpar = i % 2
        for ts_ in range(4):
            vi = v_n[0]
            bank = Vp[vi % 2]
            if vi >= 2:
                P.wait("tensor", sEvV, vi - 1)
            for hf in range(2):
                for c in range(NC):
                    lastmm = (hf == 1) and (c == NC - 1)
                    def vmm(E, o=bank[:, hf * QT:(hf + 1) * QT],
                            l=xbuf[:, (par * NC + c) * QT + ts_ * 128:
                                   (par * NC + c) * QT + ts_ * 128 + 128],
                            r=wv[:, c * D + hf * QT: c * D + (hf + 1) * QT],
                            st=(c == 0), sp=(c == NC - 1)):
                        return E.matmul(o, l, r, start=st, stop=sp)
                    if lastmm:
                        P.emit("tensor", lambda E, f=vmm: sPE.inc(f(E)))
                    else:
                        P.emit("tensor", lambda E, f=vmm: f(E))
            v_n[0] += 1
            P.wait("vector", sPE)
            if vi == 0:
                P.wait("vector", sMsk, msk_done)
            if ts_ == 0 and i >= 2:
                P.wait("vector", sVWp[tpar], 16 * 8 * (i // 2))
            P.emit("vector", lambda E, b=bank, tpar=tpar, ts_=ts_:
                   sEvV.inc(E.tensor_copy(
                       vsb[:, tpar * 4160:(tpar + 1) * 4160]
                       .rearrange("p (pr s h c) -> p pr s h c",
                                  pr=NPAIR, s=4, h=2, c=65)[:, :, ts_, :, 0:64],
                       b[:].rearrange("p (pr h c) -> p pr h c",
                                      pr=NPAIR, h=2, c=64))))
        P.wait("sync", sEvV)
        for p in range(NPAIR):
            P.emit("sync", lambda E, p=p, i=i, tpar=tpar, sv=sVWp[tpar]:
                   sv.inc(E.dma_start(
                       out=vscr[p, :, (i * 4) * 130:(i * 4 + 4) * 130],
                       in_=vsb[:, tpar * 4160 + p * 520:
                               tpar * 4160 + (p + 1) * 520]), 16))
        vdma[0] += 8
        tile_end_pe.append(sPE.n)

    for hc in range(HC):
        P.emit("gpsimd", lambda E, hc=hc: sWB.inc(E.dma_start(
            out=w1bf[hc],
            in_=w1d.rearrange("(c p) h -> p c h", p=128)[:, :, hc * 128:(hc + 1) * 128]), 16))
    for dc in range(NC):
        P.emit("gpsimd", lambda E, dc=dc: sWB.inc(E.dma_start(
            out=w2bf[dc],
            in_=w2d.rearrange("(c p) d -> p c d", p=128)[:, :, dc * 128:(dc + 1) * 128]), 16))
    wb_done = sWB.n

    p1_kw = (sKWp[0].n, sKWp[1].n)
    p1_vw = (sVWp[0].n, sVWp[1].n)
    p1_evkq = sEvKQ.n
    p1_evv = sEvV.n

    if debug:
        P.wait("gpsimd", sEvKQ, p1_evkq)
        for p in range(NPAIR):
            P.emit("gpsimd", lambda E, p=p: sOutG.inc(E.dma_start(
                out=dbg["dbg_q"][p * 128:(p + 1) * 128, :],
                in_=qT[:, p * TOK:(p + 1) * TOK]), 16))
        P.wait("gpsimd", sKWp[0], p1_kw[0])
        P.wait("gpsimd", sKWp[1], p1_kw[1])
        P.wait("gpsimd", sVWp[0], p1_vw[0])
        P.wait("gpsimd", sVWp[1], p1_vw[1])
        for p in range(NPAIR):
            P.emit("gpsimd", lambda E, p=p: sOutG.inc(E.dma_start(
                out=dbg["dbg_k"][p * 128:(p + 1) * 128, :], in_=kscr[p, :, :]), 16))
            P.emit("gpsimd", lambda E, p=p: sOutG.inc(E.dma_start(
                out=dbg["dbg_v"][p * 128:(p + 1) * 128, :], in_=vscr[p, :, :]), 16))

    ckpt(2)
    p1s.close()
    fls = ExitStack()
    sbf = lambda nm, shp, dt: fls.enter_context(nc.sbuf_tensor(nm, shp, dt))
    psf = lambda nm, shp, dt: fls.enter_context(nc.psum_tensor(nm, shp, dt))
    kst = sbf("kst", [128, 2 * S], BF16)
    vst = sbf("vst", [128, 2 * NKT * 130], BF16)
    pT = sbf("pT", [128, 2 * SET_COLS], BF16)
    ctmp = sbf("ctmp", [128, 2 * QT], BF16)
    SAp = [psf("sa0", [128, SET_COLS], F32), psf("sa1", [128, SET_COLS], F32)]
    CHp = [psf("ch0", [128, QT], F32), psf("ch1", [128, QT], F32)]

    P.wait("sync", sKWp[0], p1_kw[0])
    P.wait("sync", sKWp[1], p1_kw[1])
    P.wait("sync", sVWp[0], p1_vw[0])
    P.wait("sync", sVWp[1], p1_vw[1])
    P.wait("sync", sPE, tile_end_pe[-1])
    P.wait("sync", sEvKQ, p1_evkq)
    P.wait("sync", sEvV, p1_evv)

    pair_end_av = []

    def stage_pair(p):
        par = p % 2
        if p >= 2:
            P.wait("sync", sAV, pair_end_av[p - 2])
        P.emit("sync", lambda E, p=p, par=par, ss=sStgp[par]: ss.inc(
            E.dma_start(out=kst[:, par * S:(par + 1) * S], in_=kscr[p, :, :]), 16))
        P.emit("sync", lambda E, p=p, par=par, ss=sStgp[par]: ss.inc(
            E.dma_start(out=vst[:, par * NKT * 130:(par + 1) * NKT * 130],
                        in_=vscr[p, :, :]), 16))

    stage_pair(0)
    stage_pair(1)

    P.wait("vector", sEvV)
    P.wait("vector", sEvKQ)
    P.emit("vector", lambda E: E.memset(SAp[0][:], 0.0))
    P.emit("vector", lambda E: sCev.inc(E.memset(SAp[1][:], 0.0)))

    def slot_sets(s_):
        qs = []
        nk = 16 * s_ + 16
        for kt in range(min(nk, 16 * s_)):
            for h in range(2):
                qs.append((h, kt, 0, QT))
        for h in range(2):
            for kt in range(16 * s_, min(nk, FLASH_KT_LIMIT)):
                q0 = 32 * (kt - 16 * s_)
                qs.append((h, kt, q0, QT - q0))
        sets = []
        cur, bank_i, bank_used, bank_h = [], 0, 0, None
        for (h, kt, q0, n) in qs:
            if bank_used + n > QT or (bank_h is not None and bank_h != h):
                bank_i += 1
                bank_used = 0
                bank_h = None
            if bank_i == 3:
                sets.append(cur)
                cur, bank_i, bank_used, bank_h = [], 0, 0, None
            cur.append((h, kt, q0, n, bank_i * QT + bank_used))
            bank_used += n
            bank_h = h
        if cur:
            sets.append(cur)
        return sets

    g_set = [0]
    mask_cnt = [0]
    mask_hist = {-1: 0, 0: 0}
    lw_n = [0]

    pending_av = [None]
    for p in range(min(NPAIR, FLASH_PAIRS)):
        par = p % 2
        P.wait("tensor", sStgp[p % 2], 32 * (p // 2 + 1))
        if p == 0:
            P.wait("tensor", sMsk, msk_done)
            P.wait("scalar", sMsk, msk_done)
            P.wait("tensor", sCev)
        for s_ in range(min(NSLOT, FLASH_SLOTS)):
            sets = slot_sets(s_)
            first_av = {0: True, 1: True}
            cev_target = sCev.n
            n_sets = len(sets)

            def emit_av(g, st_, buf, av_meta,
                        first_av=first_av, cev_target=cev_target,
                        par=par, s_=s_, p=p):
                if mask_hist[g] > mask_hist.get(g - 1, 0):
                    P.wait("tensor", sMa, mask_hist[g])
                else:
                    P.wait("tensor", sExp, g + 1)
                if av_meta["first"]:
                    P.wait("tensor", sCev, cev_target)
                for qi, (h, kt, q0, n, off) in enumerate(st_):
                    stt = first_av[h]
                    first_av[h] = False
                    last_q = qi == len(st_) - 1
                    stp = kt == 16 * s_ + 15
                    def avmm(E, h=h, kt=kt, q0=q0, n=n, off=off, buf=buf,
                             par=par, stt=stt, stp=stp):
                        return E.matmul(
                            CHp[h][0:65, q0:QT],
                            vst[:, par * NKT * 130 + kt * 130 + h * 65:
                                par * NKT * 130 + kt * 130 + h * 65 + 65],
                            pT[:, buf * SET_COLS + off: buf * SET_COLS + off + n],
                            start=stt, stop=stp)
                    if last_q:
                        P.emit("tensor", lambda E, f=avmm: sAV.inc(f(E)))
                    else:
                        P.emit("tensor", lambda E, f=avmm: f(E))

            for si, st_ in enumerate(sets):
                if g_set[0] >= FLASH_SETS:
                    raise _StopBuild(nc, P)
                g = g_set[0]
                buf = g % 2
                if g >= 2:
                    P.wait("tensor", sExp, g - 1)
                for qi, (h, kt, q0, n, off) in enumerate(st_):
                    last = qi == len(st_) - 1
                    def smm(E, h=h, kt=kt, q0=q0, n=n, off=off, buf=buf,
                            p=p, s_=s_, par=par):
                        return E.matmul(
                            SAp[buf][:, off:off + n],
                            kst[h * 64:(h + 1) * 64,
                                par * S + kt * 128: par * S + (kt + 1) * 128],
                            qT[h * 64:(h + 1) * 64,
                               p * TOK + s_ * QT + q0: p * TOK + (s_ + 1) * QT],
                            start=True, stop=True)
                    if last:
                        P.emit("tensor", lambda E, f=smm: sS.inc(f(E)))
                    else:
                        P.emit("tensor", lambda E, f=smm: f(E))
                if pending_av[0] is not None:
                    pending_av[0]()
                P.wait("scalar", sS, g + 1)
                if g >= 2:
                    P.wait("scalar", sAV, g - 1)
                    P.wait("scalar", sMa, mask_hist[g - 2])
                segs = {}
                for (h_, kt_, q0_, n_, off_) in st_:
                    bnk = off_ // QT
                    lo, hi = segs.get(bnk, (off_, off_))
                    segs[bnk] = (min(lo, off_), max(hi, off_ + n_))
                seg_list = []
                for lo, hi in sorted(segs.values()):
                    if seg_list and lo - seg_list[-1][1] < 600:
                        seg_list[-1] = (seg_list[-1][0], hi)
                    else:
                        seg_list.append((lo, hi))
                for gi, (lo, hi) in enumerate(seg_list):
                    last_seg = gi == len(seg_list) - 1
                    def expop(E, buf=buf, lo=lo, hi=hi):
                        return E.activation(
                            out=pT[:, buf * SET_COLS + lo: buf * SET_COLS + hi],
                            in_=SAp[buf][:, lo:hi], func=AF.Exp, scale=SMSCALE)
                    if last_seg:
                        P.emit("scalar", lambda E, f=expop: sExp.inc(f(E)))
                    else:
                        P.emit("scalar", lambda E, f=expop: f(E))
                P.wait("vector", sExp, g + 1)
                for (h, kt, q0, n, off) in st_:
                    b_ = kt - 16 * s_
                    if b_ >= 0:
                        P.emit("vector", lambda E, buf=buf, off=off, n=n, b_=b_, q0=q0:
                               sMa.inc(E.tensor_tensor(
                                   pT[:, buf * SET_COLS + off: buf * SET_COLS + off + n],
                                   pT[:, buf * SET_COLS + off: buf * SET_COLS + off + n],
                                   masks[:, b_ * QT + q0:(b_ + 1) * QT],
                                   AluOpType.mult)))
                        mask_cnt[0] += 1
                mask_hist[g] = mask_cnt[0]
                av_meta = {"first": first_av[0]}
                pending_av[0] = (lambda g=g, st_=st_, buf=buf, av_meta=av_meta,
                                 f=emit_av: f(g, st_, buf, av_meta))
                g_set[0] += 1
            if not FLASH_EVAC:
                raise _StopBuild(nc, P)
            P.wait("vector", sAV, g_set[0])
            n_ev = lw_n[0]
            cpar = n_ev % 2
            if n_ev >= 2:
                P.wait("vector", sCWp[cpar], 16 * (n_ev // 2))
                P.wait("vector", sLWp[cpar], 32 * (n_ev // 2))
            r = p * NSLOT + s_
            P.emit("vector", lambda E, cpar=cpar: sCev.inc(E.tensor_copy(
                ctmp[0:64, cpar * QT:(cpar + 1) * QT], CHp[0][0:64, :])))
            P.emit("vector", lambda E, cpar=cpar: sCev.inc(E.tensor_copy(
                ctmp[64:128, cpar * QT:(cpar + 1) * QT], CHp[1][0:64, :])))
            P.emit("vector", lambda E, cpar=cpar: sCev.inc(E.tensor_copy(
                lrow[0:1, cpar * QT:(cpar + 1) * QT], CHp[0][64:65, :])))
            P.emit("vector", lambda E, cpar=cpar: sCev.inc(E.tensor_copy(
                lrow[32:33, cpar * QT:(cpar + 1) * QT], CHp[1][64:65, :])))
            P.wait("sync", sCev)
            P.emit("sync", lambda E, p=p, s_=s_, cpar=cpar, sc=sCWp[cpar]:
                   sc.inc(E.dma_start(
                       out=ctxscr[p, :, s_ * QT:(s_ + 1) * QT],
                       in_=ctmp[:, cpar * QT:(cpar + 1) * QT]), 16))
            P.emit("sync", lambda E, r=r, cpar=cpar, sl=sLWp[cpar]: sl.inc(E.dma_start(
                out=lscr[0, r, :], in_=lrow[0:1, cpar * QT:(cpar + 1) * QT]), 16))
            P.emit("sync", lambda E, r=r, cpar=cpar, sl=sLWp[cpar]: sl.inc(E.dma_start(
                out=lscr[1, r, :], in_=lrow[32:33, cpar * QT:(cpar + 1) * QT]), 16))
            lw_n[0] += 1
        pair_end_av.append(g_set[0])
        if p + 2 < NPAIR:
            stage_pair(p + 2)

    if pending_av[0] is not None:
        pending_av[0]()
    flash_cev = sCev.n
    flash_lw = (sLWp[0].n, sLWp[1].n)
    flash_cw = (sCWp[0].n, sCWp[1].n)

    ckpt(3)
    fls.close()
    eraQ.close()

    eraU = ExitStack()
    u1 = eraU.enter_context(nc.sbuf_tensor("u1", [128, NC * TOK], F32))

    p3s = ExitStack()
    sb3 = lambda nm, shp, dt: p3s.enter_context(nc.sbuf_tensor(nm, shp, dt))
    ps3 = lambda nm, shp, dt: p3s.enter_context(nc.psum_tensor(nm, shp, dt))
    ltile = sb3("ltile", [32, QT], F32)
    ltile2 = sb3("ltile2", [32, QT], F32)
    rl33 = sb3("rl33", [33, 2 * 4096], F32)
    LBsb = sb3("LBsb", [128, 4096], F32)
    cstage = sb3("cstage", [128, 2 * NC * QT], BF16)
    xot = sb3("xot", [128, 4 * QT], F32)
    u1t = sb3("u1t", [128, 2 * QT], F32)
    LBp = ps3("LBp", [128, 2048], F32)
    WOp = [ps3("wop0", [128, QT], F32), ps3("wop1", [128, QT], F32)]

    if debug:
        P.wait("gpsimd", sCWp[0], flash_cw[0])
        P.wait("gpsimd", sCWp[1], flash_cw[1])
        P.wait("gpsimd", sLWp[0], flash_lw[0])
        P.wait("gpsimd", sLWp[1], flash_lw[1])
        for p in range(NPAIR):
            P.emit("gpsimd", lambda E, p=p: sOutG.inc(E.dma_start(
                out=dbg["dbg_ctx"][p * 128:(p + 1) * 128, :],
                in_=ctxscr[p, :, :]), 16))
        P.emit("gpsimd", lambda E: sOutG.inc(E.dma_start(
            out=dbg["dbg_l"][:], in_=lscr[:, :, :]), 16))

    P.wait("sync", sLWp[0], flash_lw[0])
    P.wait("sync", sLWp[1], flash_lw[1])
    if debug:
        P.wait("sync", sOutG)
    P.emit("sync", lambda E: sLd.inc(E.dma_start(out=ltile[:], in_=lscr[0, :, :]), 16))
    P.emit("sync", lambda E: sLd.inc(E.dma_start(out=ltile2[:], in_=lscr[1, :, :]), 16))
    P.wait("vector", sLd)
    P.emit("vector", lambda E: sRe.inc(E.reciprocal(ltile[:], ltile[:])))
    P.emit("vector", lambda E: sRe.inc(E.reciprocal(ltile2[:], ltile2[:])))
    P.emit("vector", lambda E: sRe.inc(E.memset(rl33[:], 0.0)))
    P.wait("sync", sRe)
    P.emit("sync", lambda E: sLdW.inc(E.dma_start(out=lscr[0], in_=ltile[:]), 16))
    P.emit("sync", lambda E: sLdW.inc(E.dma_start(out=lscr[1], in_=ltile2[:]), 16))
    recip_w = sLdW.n

    P.wait("sync", sLdW, recip_w)
    P.wait("sync", sCWp[0], flash_cw[0])
    P.wait("sync", sCWp[1], flash_cw[1])
    P.wait("tensor", sW, w_ready)
    P.wait("scalar", sW, w_ready)
    wo_n = [0]
    ld_n = [0]
    xo_n = [0]
    for k in range(2):
        P.emit("sync", lambda E, k=k, sx=sXOp[k]: sx.inc(E.dma_start(
            out=xot[:, k * QT:(k + 1) * QT],
            in_=xTo[k * 128:(k + 1) * 128, 0:QT]), 16))
        xo_n[0] += 1
    for tt in range(TOK // QT):
        par = tt % 2
        if tt >= 2:
            P.wait("sync", sWoPE, 8 * (tt - 1))
            P.wait("sync", sNrmM, 8 * (tt - 1))
        P.emit("sync", lambda E, tt=tt, par=par, sc=sCSp[par]:
               sc.inc(E.dma_start(
                   out=cstage.rearrange("p (b cc q) -> p b cc q",
                                        b=2, cc=NC)[:, par],
                   in_=ctxscr[:, :, tt * QT:(tt + 1) * QT]
                       .rearrange("cc p q -> p cc q")), 16))
        P.emit("sync", lambda E, tt=tt, par=par, sl=sLdp[par]: sl.inc(E.dma_start(
            out=rl33[0:1, par * 4096:(par + 1) * 4096]
                .rearrange("p (r q) -> p r q", r=NC),
            in_=lscr[0].rearrange("(c s) q -> c s q", s=NSLOT)[:, tt]), 16))
        P.emit("sync", lambda E, tt=tt, par=par, sl=sLdp[par]: sl.inc(E.dma_start(
            out=rl33[32:33, par * 4096:(par + 1) * 4096]
                .rearrange("p (r q) -> p r q", r=NC),
            in_=lscr[1].rearrange("(c s) q -> c s q", s=NSLOT)[:, tt]), 16))
        P.wait("tensor", sLdp[par])
        P.wait("tensor", sRe, 3)
        for w in range(2):
            P.wait("tensor", sNrmC, 2 * tt + w)
            for k in range(4):
                cc = w * 4 + k
                P.emit("tensor", lambda E, cc=cc, k=k, par=par: sNrmM.inc(E.matmul(
                    LBp[:, k * QT:(k + 1) * QT], sel33[:],
                    rl33[0:33, par * 4096 + cc * QT: par * 4096 + (cc + 1) * QT],
                    start=True, stop=True)))
            P.wait("vector", sNrmM)
            P.emit("vector", lambda E, w=w: sNrmC.inc(E.tensor_copy(
                LBsb[:, w * 2048:(w + 1) * 2048], LBp[:])))
        P.wait("vector", sNrmC, 2 * (tt + 1))
        P.wait("vector", sCSp[par], 16 * (tt // 2 + 1))
        P.emit("vector", lambda E, par=par: sNrmX.inc(E.tensor_tensor(
            cstage[:, par * NC * QT:(par + 1) * NC * QT],
            cstage[:, par * NC * QT:(par + 1) * NC * QT],
            LBsb[:], AluOpType.mult)))
        stage_done = sNrmX.n
        for c in range(NC):
            i = wo_n[0]
            bank = WOp[i % 2]
            nxt = i + 2
            if nxt < 32:
                ntt, ncz = nxt // NC, nxt % NC
                if xo_n[0] >= 4:
                    P.wait("sync", sU1, xo_n[0] - 3)
                P.emit("sync", lambda E, ncz=ncz, ntt=ntt, sl=(xo_n[0] % 4),
                       sx=sXOp[xo_n[0] % 4]:
                       sx.inc(E.dma_start(
                           out=xot[:, sl * QT:(sl + 1) * QT],
                           in_=xTo[ncz * 128:(ncz + 1) * 128,
                                   ntt * QT:(ntt + 1) * QT]), 16))
                xo_n[0] += 1
            if i >= 2:
                P.wait("tensor", sWoEv, i - 1)
            P.wait("tensor", sNrmX, stage_done)
            for cc in range(NC):
                lastmm = cc == NC - 1
                def wmm(E, o=bank[:], l=woS[:, cc * D + c * 128: cc * D + (c + 1) * 128],
                        r=cstage[:, (par * NC + cc) * QT:(par * NC + cc + 1) * QT],
                        st=(cc == 0), sp=lastmm):
                    return E.matmul(o, l, r, start=st, stop=sp)
                if lastmm:
                    P.emit("tensor", lambda E, f=wmm: sWoPE.inc(f(E)))
                else:
                    P.emit("tensor", lambda E, f=wmm: f(E))
            wo_n[0] += 1
            P.wait("scalar", sWoPE)
            if i >= 2:
                P.wait("scalar", sU1, i - 1)
            P.emit("scalar", lambda E, bank=bank, c=c, i=i: sWoEv.inc(E.activation(
                out=u1t[:, (i % 2) * QT:(i % 2 + 1) * QT], in_=bank[:],
                func=AF.Identity, bias=bocS[:, c:c + 1])))
            P.wait("vector", sWoEv)
            P.wait("vector", sXOp[i % 4], 16 * (i // 4 + 1))
            P.emit("vector", lambda E, c=c, tt=tt, i=i: sU1.inc(E.tensor_tensor(
                u1[:, c * TOK + tt * QT: c * TOK + (tt + 1) * QT],
                u1t[:, (i % 2) * QT:(i % 2 + 1) * QT],
                xot[:, (i % 4) * QT:(i % 4 + 1) * QT],
                AluOpType.add)))
    u1_done = sU1.n

    if debug:
        P.wait("sync", sU1, u1_done)
        for c in range(NC):
            P.emit("sync", lambda E, c=c: sOut.inc(E.dma_start(
                out=dbg["dbg_u1"][c * 128:(c + 1) * 128, :],
                in_=u1[:, c * TOK:(c + 1) * TOK]), 16))
        P.wait("vector", sOut)

    ckpt(4)
    p3s.close()

    ln_id = [0]

    def emit_ln(src, dst_bf16, out_dram, ready_sem, ready_val, dbg_out=None):
        ln_id[0] += 1
        tag = f"_{ln_id[0]}"
        ln = ExitStack()
        sbl = lambda nm, shp, dt: ln.enter_context(nc.sbuf_tensor(nm + tag, shp, dt))
        psl = lambda nm, shp, dt: ln.enter_context(nc.psum_tensor(nm + tag, shp, dt))
        xsq = sbl("xsq", [128, 2 * TOK], F32)
        rm = sbl("rm", [1, TOK], F32)
        rv = sbl("rv", [1, TOK], F32)
        ra = sbl("ra", [1, TOK], F32)
        bcA = sbl("bcA", [128, TOK], F32)
        bcB = sbl("bcB", [128, TOK], F32)
        stats = psl("stats", [128, TOK], F32)
        bcp = psl("bcp", [128, TOK], F32)
        a0 = sLnA.n
        p0_ = sLnP.n
        P.wait("tensor", ready_sem, ready_val)
        P.wait("scalar", ready_sem, ready_val)
        for c in range(NC):
            xpar = c % 2
            if c >= 2:
                P.wait("scalar", sLnP, p0_ + c - 1)
            P.emit("scalar", lambda E, c=c, xpar=xpar: sLnA.inc(E.activation(
                out=xsq[:, xpar * TOK:(xpar + 1) * TOK],
                in_=src[:, c * TOK:(c + 1) * TOK], func=AF.Square)))
            for nt in range(TOK // QT):
                P.emit("tensor", lambda E, c=c, nt=nt: E.matmul(
                    stats[0:1, nt * QT:(nt + 1) * QT], ones128[:],
                    src[:, c * TOK + nt * QT: c * TOK + (nt + 1) * QT],
                    start=(c == 0), stop=(c == NC - 1)))
            P.wait("tensor", sLnA, a0 + c + 1)
            for nt in range(TOK // QT):
                lastmm = nt == TOK // QT - 1
                def qmm(E, c=c, nt=nt, xpar=xpar):
                    return E.matmul(
                        stats[32:33, nt * QT:(nt + 1) * QT], ones128[:],
                        xsq[:, xpar * TOK + nt * QT: xpar * TOK + (nt + 1) * QT],
                        start=(c == 0), stop=(c == NC - 1))
                if lastmm:
                    P.emit("tensor", lambda E, f=qmm: sLnP.inc(f(E)))
                else:
                    P.emit("tensor", lambda E, f=qmm: f(E))
        P.wait("vector", sLnP, p0_ + NC)
        P.emit("vector", lambda E: sLnR.inc(E.tensor_scalar(
            rm[:], stats[0:1, :], 1.0 / D, None, AluOpType.mult)))
        P.emit("vector", lambda E: sLnR.inc(E.tensor_scalar(
            rv[:], stats[32:33, :], 1.0 / D, None, AluOpType.mult)))
        P.wait("vector", sLnR)
        P.emit("vector", lambda E: sLnR.inc(E.tensor_tensor(
            ra[:], rm[:], rm[:], AluOpType.mult)))
        P.wait("vector", sLnR)
        P.emit("vector", lambda E: sLnR.inc(E.tensor_tensor(
            rv[:], rv[:], ra[:], AluOpType.subtract)))
        P.wait("scalar", sLnR)
        P.emit("scalar", lambda E: sLnR.inc(E.activation(
            out=ra[:], in_=rv[:], func=AF.Ln, bias=epsc[0:1, 0:1])))
        P.wait("scalar", sLnR)
        P.emit("scalar", lambda E: sLnR.inc(E.activation(
            out=ra[:], in_=ra[:], func=AF.Exp, scale=-0.5)))
        P.wait("vector", sLnR)
        P.emit("vector", lambda E: sLnR.inc(E.tensor_tensor(
            rm[:], rm[:], ra[:], AluOpType.mult)))
        P.wait("tensor", sLnR)
        for nt in range(TOK // QT):
            last = nt == TOK // QT - 1
            def bmm1(E, nt=nt):
                return E.matmul(bcp[:, nt * QT:(nt + 1) * QT], ones1[:],
                                ra[0:1, nt * QT:(nt + 1) * QT], start=True, stop=True)
            if last:
                P.emit("tensor", lambda E, f=bmm1: sLnB.inc(f(E)))
            else:
                P.emit("tensor", lambda E, f=bmm1: f(E))
        P.wait("vector", sLnB)
        P.emit("vector", lambda E: sLnB.inc(E.tensor_copy(bcA[:], bcp[:])))
        P.wait("tensor", sLnB)
        for nt in range(TOK // QT):
            last = nt == TOK // QT - 1
            def bmm2(E, nt=nt):
                return E.matmul(bcp[:, nt * QT:(nt + 1) * QT], ones1[:],
                                rm[0:1, nt * QT:(nt + 1) * QT], start=True, stop=True)
            if last:
                P.emit("tensor", lambda E, f=bmm2: sLnB.inc(f(E)))
            else:
                P.emit("tensor", lambda E, f=bmm2: f(E))
        P.wait("vector", sLnB)
        P.emit("vector", lambda E: sLnB.inc(E.tensor_copy(bcB[:], bcp[:])))
        P.wait("vector", sLnB)
        for c in range(NC):
            P.emit("vector", lambda E, c=c: sApp.inc(E.tensor_tensor(
                src[:, c * TOK:(c + 1) * TOK], src[:, c * TOK:(c + 1) * TOK],
                bcA[:], AluOpType.mult)))
            P.wait("vector", sApp)
            P.emit("vector", lambda E, c=c: sApp.inc(E.tensor_tensor(
                src[:, c * TOK:(c + 1) * TOK], src[:, c * TOK:(c + 1) * TOK],
                bcB[:], AluOpType.subtract)))
            P.wait("vector", sApp)
            if dst_bf16 is not None:
                P.emit("vector", lambda E, c=c: sApp.inc(E.tensor_copy(
                    dst_bf16[:, c * TOK:(c + 1) * TOK],
                    src[:, c * TOK:(c + 1) * TOK])))
            if out_dram is not None:
                P.wait("sync", sApp)
                P.emit("sync", lambda E, c=c: sOut.inc(E.dma_start(
                    out=out_dram[c * 128:(c + 1) * 128, :],
                    in_=src[:, c * TOK:(c + 1) * TOK]), 16))
            if dbg_out is not None:
                P.wait("sync", sApp)
                P.emit("sync", lambda E, c=c: sOut.inc(E.dma_start(
                    out=dbg_out[c * 128:(c + 1) * 128, :],
                    in_=src[:, c * TOK:(c + 1) * TOK]), 16))
        ln.close()
        return sApp.n

    eraY = ExitStack()
    y1bf = eraY.enter_context(nc.sbuf_tensor("y1bf", [128, NC * TOK], BF16))
    ln1_done = emit_ln(u1, y1bf, None, sU1, u1_done,
                       dbg_out=dbg.get("dbg_y1") if debug else None)
    if debug:
        P.wait("vector", sOut)
    y1 = u1

    ckpt(5)
    ffs = ExitStack()
    sbF = lambda nm, shp, dt: ffs.enter_context(nc.sbuf_tensor(nm, shp, dt))
    psF = lambda nm, shp, dt: ffs.enter_context(nc.psum_tensor(nm, shp, dt))
    hblk = sbF("hblk", [128, HC * QT], BF16)
    w1s = sbF("w1s", [128, 3 * 1024], BF16)
    w2s = sbF("w2s", [128, 2 * 4096], BF16)
    zgbf = sbF("zgbf", [128, 4 * ER], BF16)
    zgT = sbF("zgT", [24, QT], BF16)
    zg2bf = sbF("zg2bf", [128, 4 * ER], BF16)
    zg2T = sbF("zg2T", [24, QT], BF16)
    ge = sbF("ge", [128, 4], F32)
    gn = sbF("gn", [128, 4], F32)
    gneg = sbF("gneg", [128, 1], F32)
    u2t = sbF("u2t", [128, 2 * QT], F32)
    Z1 = psF("Z1", [128, QT], F32)
    Z2 = psF("Z2", [128, QT], F32)
    TPp = psF("TPp", [ER, 128], BF16)
    B1p = [psF("b1p0", [128, QT], F32), psF("b1p1", [128, QT], F32)]
    M2p = [psF("m2p0", [128, QT], F32), psF("m2p1", [128, QT], F32)]

    b1_n = [0]
    b2_n = [0]
    w1_ld = [0]
    w2_ld = [0]

    def load_w1(j):
        P.emit("sync", lambda E, hc=(j % HC), sl=(j % 3), sw=sFW1q[j % 3]:
               sw.inc(E.dma_start(
                   out=w1s[:, sl * 1024:(sl + 1) * 1024], in_=w1bf[hc]), 16))
        w1_ld[0] += 1

    def load_w2(j):
        P.emit("sync", lambda E, dc=(j % NC), par=(j % 2), sw=sFW2p[j % 2]:
               sw.inc(E.dma_start(
                   out=w2s[:, par * 4096:(par + 1) * 4096], in_=w2bf[dc]), 16))
        w2_ld[0] += 1

    def gates_and_zg(Z, dst_zg, dst_zgT, sub, zsem_val):
        P.wait("vector", sFz, zsem_val)
        P.emit("vector", lambda E: sFg.inc(E.reduce_max(
            gneg[:], Z[:, 24:27], AX.X, negate=True)))
        P.wait("scalar", sFg)
        P.emit("scalar", lambda E: sFg.inc(E.activation(
            out=ge[:, 0:3], in_=Z[:, 24:27],
            func=AF.Exp, bias=gneg[:, 0:1])))
        P.wait("vector", sFg)
        P.emit("vector", lambda E: sFg.inc(E.reduce_sum(
            gn[:, 0:1], ge[:, 0:3], AX.X)))
        P.wait("vector", sFg)
        P.emit("vector", lambda E: sFg.inc(E.reciprocal(gn[:, 1:2], gn[:, 0:1])))
        P.wait("vector", sFg)
        P.emit("vector", lambda E: sFg.inc(E.tensor_scalar(
            ge[:, 0:3], ge[:, 0:3], gn[:, 1:2], None, AluOpType.mult)))
        P.wait("vector", sFg)
        for e_ in range(3):
            last = e_ == 2
            def zgop(E, e_=e_, sub=sub):
                return E.tensor_scalar(
                    dst_zg[:, sub * ER + e_ * 8: sub * ER + e_ * 8 + 8],
                    Z[:, e_ * 8: e_ * 8 + 8],
                    ge[:, e_:e_ + 1], LSCALE, AluOpType.mult, AluOpType.mult)
            if last:
                P.emit("vector", lambda E, f=zgop: sFzg.inc(f(E)))
            else:
                P.emit("vector", lambda E, f=zgop: f(E))
        P.wait("tensor", sFzg)
        P.emit("tensor", lambda E, sub=sub: sFt.inc(E.transpose(
            TPp[:], dst_zg[:, sub * ER:(sub + 1) * ER], ident[:])))
        P.wait("vector", sFt)
        P.emit("vector", lambda E, sub=sub: sFt.inc(E.tensor_copy(
            dst_zgT[:, sub * 128:(sub + 1) * 128], TPp[:])))

    P.wait("sync", sWB, wb_done)
    P.wait("sync", sApp, ln1_done)
    load_w1(0)
    load_w1(1)
    P.wait("tensor", sApp, ln1_done)

    zg0 = sFzg.n
    for tb in range(TOK // QT):
        for sub in range(4):
            P.wait("tensor", sFzg, zg0 + 8 * tb + sub)
            for c in range(NC):
                lastmm = c == NC - 1
                def zmm(E, c=c, sub=sub, tb=tb):
                    return E.matmul(
                        Z1[:, 0:28],
                        y1bf[:, c * TOK + tb * QT + sub * 128:
                             c * TOK + tb * QT + (sub + 1) * 128],
                        a1r1S[:, c * 32: c * 32 + 28],
                        start=(c == 0), stop=(c == NC - 1))
                if lastmm:
                    P.emit("tensor", lambda E, f=zmm: sFz.inc(f(E)))
                else:
                    P.emit("tensor", lambda E, f=zmm: f(E))
            gates_and_zg(Z1, zgbf, zgT, sub, sFz.n)
        for hc in range(HC):
            i = b1_n[0]
            par = i % 3
            if i + 2 < 4 * HC:
                P.wait("sync", sFb1, i)
                load_w1(i + 2)
            bank = B1p[i % 2]
            if i >= 2:
                P.wait("tensor", sFh, i - 1)
            P.wait("tensor", sFW1q[par], 16 * (i // 3 + 1))
            P.wait("tensor", sFt)
            for c in range(NC):
                P.emit("tensor", lambda E, c=c, par=par, bank=bank, tb=tb:
                       E.matmul(bank[:],
                                w1s[:, par * 1024 + c * 128: par * 1024 + (c + 1) * 128],
                                y1bf[:, c * TOK + tb * QT: c * TOK + (tb + 1) * QT],
                                start=(c == 0), stop=False))
            P.emit("tensor", lambda E, hc=hc, bank=bank: sFb1.inc(E.matmul(
                bank[:], b1S[:, hc * 128:(hc + 1) * 128], zgT[:],
                start=False, stop=True)))
            b1_n[0] += 1
            P.wait("scalar", sFb1)
            P.emit("scalar", lambda E, hc=hc, bank=bank: sFh.inc(E.activation(
                out=hblk[:, hc * QT:(hc + 1) * QT], in_=bank[:],
                func=AF.Relu, bias=b1cS[:, hc:hc + 1])))
        h_done = sFh.n

        if debug and tb == 0:
            P.wait("gpsimd", sFh, h_done)
            for hc in range(HC):
                P.emit("gpsimd", lambda E, hc=hc: sOutG.inc(E.dma_start(
                    out=dbg["dbg_h"][hc * 128:(hc + 1) * 128, :],
                    in_=hblk[:, hc * QT:(hc + 1) * QT]), 16))
            P.wait("scalar", sOutG)

        P.wait("tensor", sFh, h_done)
        for sub in range(4):
            P.wait("tensor", sFzg, zg0 + 8 * tb + 4 + sub)
            for hc in range(HC):
                lastmm = hc == HC - 1
                def z2mm(E, hc=hc, sub=sub):
                    return E.matmul(
                        Z2[:, 0:28],
                        hblk[:, hc * QT + sub * 128: hc * QT + (sub + 1) * 128],
                        a2r2S[:, hc * 32: hc * 32 + 28],
                        start=(hc == 0), stop=(hc == HC - 1))
                if lastmm:
                    P.emit("tensor", lambda E, f=z2mm: sFz.inc(f(E)))
                else:
                    P.emit("tensor", lambda E, f=z2mm: f(E))
            gates_and_zg(Z2, zg2bf, zg2T, sub, sFz.n)
        if tb == 0:
            P.wait("sync", sWB, wb_done)
            load_w2(0)
        for dc in range(NC):
            i = b2_n[0]
            par = i % 2
            if i + 1 < 4 * NC:
                P.wait("sync", sFb2, i)
                load_w2(i + 1)
            bank = M2p[i % 2]
            if i >= 2:
                P.wait("tensor", sFu, i - 1)
            P.wait("tensor", sFW2p[par], 16 * (i // 2 + 1))
            P.wait("tensor", sFt)
            for hc in range(HC):
                P.emit("tensor", lambda E, hc=hc, par=par, bank=bank:
                       E.matmul(bank[:],
                                w2s[:, par * 4096 + hc * 128: par * 4096 + (hc + 1) * 128],
                                hblk[:, hc * QT:(hc + 1) * QT],
                                start=(hc == 0), stop=False))
            P.emit("tensor", lambda E, dc=dc, bank=bank: sFb2.inc(E.matmul(
                bank[:], b2S[:, dc * 128:(dc + 1) * 128], zg2T[:],
                start=False, stop=True)))
            b2_n[0] += 1
            P.wait("scalar", sFb2)
            if i >= 2:
                P.wait("scalar", sFu2, i - 1)
            P.emit("scalar", lambda E, dc=dc, bank=bank, i=i: sFu.inc(E.activation(
                out=u2t[:, (i % 2) * QT:(i % 2 + 1) * QT], in_=bank[:],
                func=AF.Identity, bias=b2cS[:, dc:dc + 1])))
            P.wait("vector", sFu)
            P.emit("vector", lambda E, dc=dc, tb=tb, i=i: sFu2.inc(E.tensor_tensor(
                y1[:, dc * TOK + tb * QT: dc * TOK + (tb + 1) * QT],
                y1[:, dc * TOK + tb * QT: dc * TOK + (tb + 1) * QT],
                u2t[:, (i % 2) * QT:(i % 2 + 1) * QT], AluOpType.add)))

    ffn_done = sFu2.n

    if debug:
        P.wait("sync", sFu2, ffn_done)
        for c in range(NC):
            P.emit("sync", lambda E, c=c: sOut.inc(E.dma_start(
                out=dbg["dbg_u2"][c * 128:(c + 1) * 128, :],
                in_=y1[:, c * TOK:(c + 1) * TOK]), 16))
        P.wait("vector", sOut)

    ckpt(6)
    ffs.close()
    eraY.close()
    emit_ln(y1, None, yout, sFu2, ffn_done)
    eraU.close()

    P.wait("sync", sOut, sOut.n)
    if debug:
        P.wait("gpsimd", sOutG, sOutG.n)
    ckpt(99)

    _emit_block(nc, P)
    return nc


def _emit_block(nc, P):
    with nc.Block() as blk:
        blk.tensor(lambda E: [f(E) for f in P.ops["tensor"]])
        blk.vector(lambda E: [f(E) for f in P.ops["vector"]])
        blk.scalar(lambda E: [f(E) for f in P.ops["scalar"]])
        blk.gpsimd(lambda E: [f(E) for f in P.ops["gpsimd"]])
        blk.sync(lambda E: [f(E) for f in P.ops["sync"]])


_NC_CACHE = {}


def _get_nc(debug=False):
    key = bool(debug)
    if key not in _NC_CACHE:
        _NC_CACHE[key] = build_nc(debug=key)
    return _NC_CACHE[key]


def _stage(inputs, debug=False):
    x = np.asarray(inputs["L0_attn_x"], dtype=np.float32)
    Wq = np.asarray(inputs["L0_attn_Wq"], np.float32)
    Wk = np.asarray(inputs["L0_attn_Wk"], np.float32)
    Wv = np.asarray(inputs["L0_attn_Wv"], np.float32)
    Wo = np.asarray(inputs["L0_attn_Wo"], np.float32)
    bo = np.asarray(inputs["L0_attn_bo"], np.float32)
    W1 = np.asarray(inputs["L0_ffn_W1"], np.float32)
    b1 = np.asarray(inputs["L0_ffn_b1"], np.float32)
    A1 = np.asarray(inputs["L0_ffn_A1"], np.float32)
    B1 = np.asarray(inputs["L0_ffn_B1"], np.float32)
    R1 = np.asarray(inputs["L0_ffn_R1"], np.float32)
    W2 = np.asarray(inputs["L0_ffn_W2"], np.float32)
    b2 = np.asarray(inputs["L0_ffn_b2"], np.float32)
    A2 = np.asarray(inputs["L0_ffn_A2"], np.float32)
    B2 = np.asarray(inputs["L0_ffn_B2"], np.float32)
    R2 = np.asarray(inputs["L0_ffn_R2"], np.float32)

    C = np.ascontiguousarray
    shared = {
        "wqT": C(Wq.T), "wkT": C(Wk.T), "wvT": C(Wv.T), "woT": C(Wo.T),
        "boc": C(bo.reshape(NC, 128).T),
        "w1": C(W1), "b1c": C(b1.reshape(HC, 128).T),
        "w2": C(W2), "b2c": C(b2.reshape(NC, 128).T),
        "a1r1": C(np.concatenate(
            [A1.transpose(1, 0, 2).reshape(D, ER), R1,
             np.zeros((D, 5), np.float32)], axis=1)),
        "b1cat": C(B1.reshape(ER, H)),
        "a2r2": C(np.concatenate(
            [A2.transpose(1, 0, 2).reshape(H, ER), R2,
             np.zeros((H, 5), np.float32)], axis=1)),
        "b2cat": C(B2.reshape(ER, D)),
    }
    xT = [C(x[b].T) for b in range(2)]
    in_maps = []
    for core in range(8):
        b, j = core // 4, core % 4
        m = dict(shared)
        m["xTb"] = xT[b]
        m["xTo"] = C(x[b, j::4, :].T)
        t = (128.0 * np.arange(16, dtype=np.float32) - j)[None, :]
        m["thr"] = C(np.broadcast_to(t, (128, 16)))
        in_maps.append(m)
    return in_maps


def kernel(**inputs):
    nc = _get_nc(debug=DEBUG)
    in_maps = _stage(inputs, debug=DEBUG)
    res = run_bass_kernel_spmd(nc, in_maps, core_ids=list(range(8)))
    x = inputs["L0_attn_x"]
    out = np.empty((2, S, D), dtype=np.float32)
    for core in range(8):
        b, j = core // 4, core % 4
        out[b, j::4, :] = res.results[core]["y"].T
    return out


# version 23
# speedup vs baseline: 1.3029x; 1.3029x over previous
import numpy as np
import concourse.bass as bass
import concourse.mybir as mybir
from concourse.bass_utils import run_bass_kernel_spmd
from concourse.alu_op_type import AluOpType
from contextlib import ExitStack

F32 = mybir.dt.float32
BF16 = mybir.dt.bfloat16
AF = mybir.ActivationFunctionType
AX = mybir.AxisListType

D = 1024
S = 8192
NPAIR = 8
NC = 8
TOK = 2048
NSLOT = 4
QT = 512
NKT = 64
H = 4096
HC = 32
ER = 24
EPS = 1e-5
SMSCALE = 0.125
LSCALE = 0.125
SET_COLS = 1536

DEBUG = False
FLASH_PAIRS = NPAIR
FLASH_SLOTS = NSLOT
FLASH_SETS = 10**9
FLASH_EVAC = True
FLASH_KT_LIMIT = 10**9


class SemT:
    def __init__(self, h):
        self.h = h
        self.n = 0

    def inc(self, instr, k=1):
        instr.then_inc(self.h, k)
        self.n += k
        return instr


class _Dummy:
    def then_inc(self, *a, **k):
        return self


class _Proxy:
    def __getattr__(self, name):
        return lambda *a, **k: _Dummy()


_PROXY = _Proxy()


class Prog:
    ENGS = ("tensor", "vector", "scalar", "gpsimd", "sync")

    def __init__(self):
        self.ops = {e: [] for e in self.ENGS}
        self.last_wait = {}

    def emit(self, eng, fn):
        fn(_PROXY)
        self.ops[eng].append(fn)

    def wait(self, eng, sem, val=None):
        v = sem.n if val is None else val
        key = (eng, id(sem))
        if v > 0 and v > self.last_wait.get(key, 0):
            self.last_wait[key] = v
            self.ops[eng].append(lambda E, s=sem.h, v=v: E.wait_ge(s, v))


class _StopBuild(Exception):
    pass


def build_nc(debug=False, phases=99):
    try:
        return _build_nc_inner(debug, phases)
    except _StopBuild as e:
        nc, P = e.args
        _emit_block(nc, P)
        return nc


def _build_nc_inner(debug=False, phases=99):
    nc = bass.Bass("TRN2", target_bir_lowering=False, debug=False)
    P = Prog()

    def ckpt(n):
        if phases < n:
            raise _StopBuild(nc, P)

    xTb = nc.dram_tensor("xTb", [D, S], F32, kind="ExternalInput").ap()
    xTo = nc.dram_tensor("xTo", [D, TOK], F32, kind="ExternalInput").ap()
    wqT = nc.dram_tensor("wqT", [D, D], F32, kind="ExternalInput").ap()
    wkT = nc.dram_tensor("wkT", [D, D], F32, kind="ExternalInput").ap()
    wvT = nc.dram_tensor("wvT", [D, D], F32, kind="ExternalInput").ap()
    woT = nc.dram_tensor("woT", [D, D], F32, kind="ExternalInput").ap()
    boc = nc.dram_tensor("boc", [128, NC], F32, kind="ExternalInput").ap()
    w1d = nc.dram_tensor("w1", [D, H], F32, kind="ExternalInput").ap()
    b1c = nc.dram_tensor("b1c", [128, HC], F32, kind="ExternalInput").ap()
    w2d = nc.dram_tensor("w2", [H, D], F32, kind="ExternalInput").ap()
    b2c = nc.dram_tensor("b2c", [128, NC], F32, kind="ExternalInput").ap()
    a1r1 = nc.dram_tensor("a1r1", [D, 32], F32, kind="ExternalInput").ap()
    b1cat = nc.dram_tensor("b1cat", [ER, H], F32, kind="ExternalInput").ap()
    a2r2 = nc.dram_tensor("a2r2", [H, 32], F32, kind="ExternalInput").ap()
    b2cat = nc.dram_tensor("b2cat", [ER, D], F32, kind="ExternalInput").ap()
    thr = nc.dram_tensor("thr", [128, 16], F32, kind="ExternalInput").ap()
    yout = nc.dram_tensor("y", [D, TOK], F32, kind="ExternalOutput").ap()

    kscr = nc.dram_tensor("kscr", [NPAIR, 128, S], BF16, kind="Internal").ap()
    vscr = nc.dram_tensor("vscr", [NPAIR, 128, NKT * 130], BF16, kind="Internal").ap()
    lscr = nc.dram_tensor("lscr", [2, 32, QT], F32, kind="Internal").ap()
    ctxscr = nc.dram_tensor("ctxscr", [NPAIR, 128, TOK], BF16, kind="Internal").ap()
    w1bf = nc.dram_tensor("w1bf", [HC, 128, 1024], BF16, kind="Internal").ap()
    w2bf = nc.dram_tensor("w2bf", [NC, 128, 4096], BF16, kind="Internal").ap()

    dbg = {}
    if debug:
        for nm, shp in [("dbg_q", [D, TOK]), ("dbg_k", [D, S]),
                        ("dbg_v", [NPAIR * 128, NKT * 130]),
                        ("dbg_ctx", [D, TOK]), ("dbg_l", [2, 32, QT]),
                        ("dbg_u1", [D, TOK]), ("dbg_y1", [D, TOK]),
                        ("dbg_h", [H, QT]), ("dbg_u2", [D, TOK])]:
            dbg[nm] = nc.dram_tensor(nm, shp, F32, kind="ExternalOutput").ap()

    root = ExitStack()
    sbR = lambda nm, shp, dt: root.enter_context(nc.sbuf_tensor(nm, shp, dt))
    mksem = lambda nm: SemT(root.enter_context(nc.semaphore(nm)))

    lrow = sbR("lrow", [33, 2 * QT], F32)
    iot = sbR("iot", [128, QT], F32)
    thrS = sbR("thrS", [128, 16], F32)
    ident = sbR("ident", [128, 128], BF16)
    sel33 = sbR("sel33", [33, 128], F32)
    ones1 = sbR("ones1", [1, 128], F32)
    ones128 = sbR("ones128", [128, 1], F32)
    epsc = sbR("epsc", [1, 1], F32)
    bocS = sbR("bocS", [128, NC], F32)
    b1cS = sbR("b1cS", [128, HC], F32)
    b2cS = sbR("b2cS", [128, NC], F32)
    a1r1S = sbR("a1r1S", [128, NC * 32], BF16)
    a2r2S = sbR("a2r2S", [128, HC * 32], BF16)
    b1S = sbR("b1S", [ER, H], BF16)
    b2S = sbR("b2S", [ER, D], BF16)
    woS = sbR("woS", [128, NC * D], BF16)

    sW = mksem("sW"); sMsk = mksem("sMsk")
    sXp = [mksem("sX0"), mksem("sX1")]
    sXF = [mksem("sXF0"), mksem("sXF1")]; sXC = mksem("sXC")
    sTile = mksem("sTile"); sPE = mksem("sPE")
    sEvKQ = mksem("sEvKQ"); sEvV = mksem("sEvV")
    sKWp = [mksem("sKW0"), mksem("sKW1")]
    sVWp = [mksem("sVW0"), mksem("sVW1")]
    sStgp = [mksem("sStg0"), mksem("sStg1")]
    sS = mksem("sS"); sExp = mksem("sExp")
    sMa = mksem("sMa"); sAV = mksem("sAV"); sCev = mksem("sCev")
    sLWp = [mksem("sLW0"), mksem("sLW1")]
    sLd = mksem("sLd"); sLdW = mksem("sLdW"); sRe = mksem("sRe"); sNrm = mksem("sNrm")
    sLdp = [mksem("sLda"), mksem("sLdb")]
    sNrmM = mksem("sNrmM"); sNrmC = mksem("sNrmC"); sNrmX = mksem("sNrmX")
    sWoPE = mksem("sWoPE"); sWoEv = mksem("sWoEv"); sU1 = mksem("sU1")
    sLnA = mksem("sLnA"); sLnP = mksem("sLnP"); sLnR = mksem("sLnR")
    sLnB = mksem("sLnB"); sApp = mksem("sApp")
    sFz = mksem("sFz"); sFg = mksem("sFg"); sFzg = mksem("sFzg")
    sFt = mksem("sFt")
    sFW1p = [mksem("sFW1a"), mksem("sFW1b")]
    sFW2p = [mksem("sFW2a"), mksem("sFW2b")]
    sFb1 = mksem("sFb1"); sFb2 = mksem("sFb2"); sFh = mksem("sFh")
    sFu = mksem("sFu"); sFu2 = mksem("sFu2")
    sCWp = [mksem("sCW0"), mksem("sCW1")]
    sCSp = [mksem("sCS0"), mksem("sCS1")]
    sXOp = [mksem("sXO0"), mksem("sXO1"), mksem("sXO2"), mksem("sXO3")]
    sOutG = mksem("sOutG")
    sWB = mksem("sWB")
    sFW1q = [mksem("sFW1q0"), mksem("sFW1q1"), mksem("sFW1q2")]
    sOut = mksem("sOut")

    eraQ = ExitStack()
    qT = eraQ.enter_context(nc.sbuf_tensor("qT", [128, NPAIR * TOK], BF16))
    masks = eraQ.enter_context(nc.sbuf_tensor("masks", [128, 16 * QT], BF16))

    def cast_load(dst_ap, src_ap):
        P.emit("gpsimd", lambda E, o=dst_ap, i=src_ap:
               sW.inc(E.dma_start(out=o, in_=i), 16))

    p1s = ExitStack()
    sb1 = lambda nm, shp, dt: p1s.enter_context(nc.sbuf_tensor(nm, shp, dt))
    ps1 = lambda nm, shp, dt: p1s.enter_context(nc.psum_tensor(nm, shp, dt))
    wq = sb1("wq", [128, NC * D], BF16)
    wk = sb1("wk", [128, NC * D], BF16)
    wv = sb1("wv", [128, NC * D], BF16)
    xbuf = sb1("xbuf", [128, 2 * NC * 2 * QT], BF16)
    kbuf = sb1("kbuf", [128, NPAIR * 2 * QT], BF16)
    vsb = sb1("vsb", [128, 2 * 4160], BF16)
    KQp = [ps1("kqp0", [128, QT], F32), ps1("kqp1", [128, QT], F32)]
    Vp = [ps1("vp0", [128, 1024], F32), ps1("vp1", [128, 1024], F32)]

    P.emit("gpsimd", lambda E: sW.inc(E.dma_start(out=thrS[:], in_=thr[:]), 16))
    P.emit("gpsimd", lambda E: sW.inc(E.dma_start(out=bocS[:], in_=boc[:]), 16))
    P.emit("gpsimd", lambda E: sW.inc(E.dma_start(out=b1cS[:], in_=b1c[:]), 16))
    P.emit("gpsimd", lambda E: sW.inc(E.dma_start(out=b2cS[:], in_=b2c[:]), 16))
    thr_ready = sW.n
    for c in range(NC):
        cast_load(wq[:, c * D:(c + 1) * D], wqT[c * 128:(c + 1) * 128, :])
    wq_ready = sW.n
    for g0 in range(2):
        for c in range(NC):
            P.emit("gpsimd", lambda E,
                   o=xbuf[:, g0 * 8192 + c * 1024: g0 * 8192 + (c + 1) * 1024],
                   i_=xTo[c * 128:(c + 1) * 128, g0 * 1024:(g0 + 1) * 1024],
                   sx=sXp[g0]:
                   sx.inc(E.dma_start(out=o, in_=i_), 16))
    for c in range(NC):
        cast_load(wk[:, c * D:(c + 1) * D], wkT[c * 128:(c + 1) * 128, :])
        cast_load(wv[:, c * D:(c + 1) * D], wvT[c * 128:(c + 1) * 128, :])
    wkv_ready = sW.n

    P.emit("gpsimd", lambda E: sMsk.inc(E.iota(
        iot[:], pattern=[[4, QT]], base=0, channel_multiplier=-1,
        allow_small_or_imprecise_dtypes=True)))
    P.wait("vector", sW, thr_ready)
    P.wait("vector", sMsk)
    for b_ in range(16):
        P.emit("vector", lambda E, b_=b_: sMsk.inc(E.tensor_scalar(
            masks[:, b_ * QT:(b_ + 1) * QT], iot[:], thrS[:, b_:b_ + 1], None,
            AluOpType.is_ge)))
    P.wait("gpsimd", sMsk)
    P.emit("gpsimd", lambda E: sMsk.inc(E.iota(
        iot[:], pattern=[[1, QT]], base=0, channel_multiplier=-1,
        allow_small_or_imprecise_dtypes=True)))
    P.wait("vector", sMsk)
    P.emit("vector", lambda E: sMsk.inc(E.tensor_scalar(
        ident[:], iot[:, 0:128], 0.0, None, AluOpType.is_equal)))
    P.emit("vector", lambda E: E.memset(sel33[:], 0.0))
    P.emit("vector", lambda E: E.drain())
    P.emit("vector", lambda E: E.memset(sel33[0:1, 0:64], 1.0))
    P.emit("vector", lambda E: E.memset(sel33[32:33, 64:128], 1.0))
    P.emit("vector", lambda E: E.memset(ones1[:], 1.0))
    P.emit("vector", lambda E: E.memset(ones128[:], 1.0))
    P.emit("vector", lambda E: E.memset(epsc[:], EPS))
    P.emit("vector", lambda E: sMsk.inc(E.memset(vsb[:], 1.0)))
    msk_done = sMsk.n

    kq_n = [0]; v_n = [0]; tile_n = [0]; kdma = [0]; vdma = [0]
    tile_end_pe = []

    def load_xgroup(src2d, base):
        g = tile_n[0] // 2
        gpar = g % 2
        if g >= 2:
            P.wait("gpsimd", sPE, tile_end_pe[2 * g - 3])
        for c in range(NC):
            P.emit("gpsimd", lambda E,
                   o=xbuf[:, gpar * 8192 + c * 1024: gpar * 8192 + (c + 1) * 1024],
                   i_=src2d[c * 128:(c + 1) * 128, base: base + 1024],
                   sx=sXp[gpar]:
                   sx.inc(E.dma_start(out=o, in_=i_), 16))

    def xcol(gpar, tp, c):
        return gpar * 8192 + c * 1024 + tp * QT

    def kq_proj(gpar, tp, wmat, dst_fn):
        for p in range(NPAIR):
            idx = kq_n[0]
            bank = KQp[idx % 2]
            if idx >= 2:
                P.wait("tensor", sEvKQ, idx - 1)
            for c in range(NC):
                lastmm = c == NC - 1
                def mm(E, o=bank[:], l=wmat[:, c * D + p * 128: c * D + (p + 1) * 128],
                       r=xbuf[:, xcol(gpar, tp, c): xcol(gpar, tp, c) + QT],
                       st=(c == 0), sp=lastmm):
                    return E.matmul(o, l, r, start=st, stop=sp)
                if lastmm:
                    P.emit("tensor", lambda E, f=mm: sPE.inc(f(E)))
                else:
                    P.emit("tensor", lambda E, f=mm: f(E))
            kq_n[0] += 1
            P.wait("vector", sPE)
            dst_fn(p, bank)

    for i in range(TOK // QT):
        gpar = (tile_n[0] // 2) % 2
        tp = tile_n[0] % 2
        P.wait("tensor", sXp[gpar])
        if i == 0:
            P.wait("tensor", sW, wq_ready)

        def qdst(p, bank, i=i):
            P.emit("vector", lambda E,
                   o=qT[:, p * TOK + i * QT: p * TOK + (i + 1) * QT],
                   i_=bank[:]: sEvKQ.inc(E.tensor_copy(o, i_)))
        kq_proj(gpar, tp, wq, qdst)
        tile_n[0] += 1
        tile_end_pe.append(sPE.n)

    for i in range(S // QT):
        gpar = (tile_n[0] // 2) % 2
        tp = tile_n[0] % 2
        if tp == 0:
            load_xgroup(xTb, (i // 2) * 1024)
        P.wait("tensor", sXp[gpar])
        if i == 0:
            P.wait("tensor", sW, wkv_ready)

        def kdst(p, bank, i=i):
            g_, tp = i // 2, i % 2
            if g_ >= 1:
                P.wait("vector", sKWp[0], 16 * (NPAIR * (g_ - 1) + p + 1))
            P.emit("vector", lambda E,
                   o=kbuf[:, p * 1024 + tp * QT: p * 1024 + tp * QT + QT],
                   i_=bank[:]: sEvKQ.inc(E.tensor_copy(o, i_)))
            kdma[0] += 1
        kq_proj(gpar, tp, wk, kdst)
        if i % 2 == 1:
            P.wait("sync", sEvKQ)
            for p in range(NPAIR):
                P.emit("sync", lambda E, p=p, i=i: sKWp[0].inc(E.dma_start(
                    out=kscr[p, :, (i - 1) * QT:(i + 1) * QT],
                    in_=kbuf[:, p * 1024:(p + 1) * 1024]), 16))

        tpar = i % 2
        for ts_ in range(4):
            vi = v_n[0]
            bank = Vp[vi % 2]
            if vi >= 2:
                P.wait("tensor", sEvV, vi - 1)
            for hf in range(2):
                for c in range(NC):
                    lastmm = (hf == 1) and (c == NC - 1)
                    def vmm(E, o=bank[:, hf * QT:(hf + 1) * QT],
                            l=xbuf[:, xcol(gpar, tp, c) + ts_ * 128:
                                   xcol(gpar, tp, c) + ts_ * 128 + 128],
                            r=wv[:, c * D + hf * QT: c * D + (hf + 1) * QT],
                            st=(c == 0), sp=(c == NC - 1)):
                        return E.matmul(o, l, r, start=st, stop=sp)
                    if lastmm:
                        P.emit("tensor", lambda E, f=vmm: sPE.inc(f(E)))
                    else:
                        P.emit("tensor", lambda E, f=vmm: f(E))
            v_n[0] += 1
            P.wait("vector", sPE)
            if vi == 0:
                P.wait("vector", sMsk, msk_done)
            if ts_ == 0 and i >= 2:
                P.wait("vector", sVWp[tpar], 16 * 8 * (i // 2))
            P.emit("vector", lambda E, b=bank, tpar=tpar, ts_=ts_:
                   sEvV.inc(E.tensor_copy(
                       vsb[:, tpar * 4160:(tpar + 1) * 4160]
                       .rearrange("p (pr s h c) -> p pr s h c",
                                  pr=NPAIR, s=4, h=2, c=65)[:, :, ts_, :, 0:64],
                       b[:].rearrange("p (pr h c) -> p pr h c",
                                      pr=NPAIR, h=2, c=64))))
        P.wait("sync", sEvV)
        for p in range(NPAIR):
            P.emit("sync", lambda E, p=p, i=i, tpar=tpar, sv=sVWp[tpar]:
                   sv.inc(E.dma_start(
                       out=vscr[p, :, (i * 4) * 130:(i * 4 + 4) * 130],
                       in_=vsb[:, tpar * 4160 + p * 520:
                               tpar * 4160 + (p + 1) * 520]), 16))
        vdma[0] += 8
        tile_n[0] += 1
        tile_end_pe.append(sPE.n)

    for c in range(NC):
        cast_load(woS[:, c * D:(c + 1) * D], woT[c * 128:(c + 1) * 128, :])
        cast_load(a1r1S[:, c * 32:(c + 1) * 32], a1r1[c * 128:(c + 1) * 128, :])
    for c in range(HC):
        cast_load(a2r2S[:, c * 32:(c + 1) * 32], a2r2[c * 128:(c + 1) * 128, :])
    cast_load(b1S[:], b1cat[:])
    cast_load(b2S[:], b2cat[:])
    w_ready = sW.n

    for hc in range(HC):
        P.emit("gpsimd", lambda E, hc=hc: sWB.inc(E.dma_start(
            out=w1bf[hc],
            in_=w1d.rearrange("(c p) h -> p c h", p=128)[:, :, hc * 128:(hc + 1) * 128]), 16))
    for dc in range(NC):
        P.emit("gpsimd", lambda E, dc=dc: sWB.inc(E.dma_start(
            out=w2bf[dc],
            in_=w2d.rearrange("(c p) d -> p c d", p=128)[:, :, dc * 128:(dc + 1) * 128]), 16))
    wb_done = sWB.n

    p1_kw = (sKWp[0].n, sKWp[1].n)
    p1_vw = (sVWp[0].n, sVWp[1].n)
    p1_evkq = sEvKQ.n
    p1_evv = sEvV.n

    if debug:
        P.wait("gpsimd", sEvKQ, p1_evkq)
        for p in range(NPAIR):
            P.emit("gpsimd", lambda E, p=p: sOutG.inc(E.dma_start(
                out=dbg["dbg_q"][p * 128:(p + 1) * 128, :],
                in_=qT[:, p * TOK:(p + 1) * TOK]), 16))
        P.wait("gpsimd", sKWp[0], p1_kw[0])
        P.wait("gpsimd", sKWp[1], p1_kw[1])
        P.wait("gpsimd", sVWp[0], p1_vw[0])
        P.wait("gpsimd", sVWp[1], p1_vw[1])
        for p in range(NPAIR):
            P.emit("gpsimd", lambda E, p=p: sOutG.inc(E.dma_start(
                out=dbg["dbg_k"][p * 128:(p + 1) * 128, :], in_=kscr[p, :, :]), 16))
            P.emit("gpsimd", lambda E, p=p: sOutG.inc(E.dma_start(
                out=dbg["dbg_v"][p * 128:(p + 1) * 128, :], in_=vscr[p, :, :]), 16))

    ckpt(2)
    p1s.close()
    fls = ExitStack()
    sbf = lambda nm, shp, dt: fls.enter_context(nc.sbuf_tensor(nm, shp, dt))
    psf = lambda nm, shp, dt: fls.enter_context(nc.psum_tensor(nm, shp, dt))
    kst = sbf("kst", [128, 2 * S], BF16)
    vst = sbf("vst", [128, 2 * NKT * 130], BF16)
    pT = sbf("pT", [128, 2 * SET_COLS], BF16)
    ctmp = sbf("ctmp", [128, 2 * QT], BF16)
    SAp = [psf("sa0", [128, SET_COLS], F32), psf("sa1", [128, SET_COLS], F32)]
    CHp = [psf("ch0", [128, QT], F32), psf("ch1", [128, QT], F32)]

    P.wait("sync", sKWp[0], p1_kw[0])
    P.wait("sync", sKWp[1], p1_kw[1])
    P.wait("sync", sVWp[0], p1_vw[0])
    P.wait("sync", sVWp[1], p1_vw[1])
    P.wait("sync", sPE, tile_end_pe[-1])
    P.wait("sync", sEvKQ, p1_evkq)
    P.wait("sync", sEvV, p1_evv)

    pair_end_av = []

    def stage_pair(p):
        par = p % 2
        if p >= 2:
            P.wait("sync", sAV, pair_end_av[p - 2])
        P.emit("sync", lambda E, p=p, par=par, ss=sStgp[par]: ss.inc(
            E.dma_start(out=kst[:, par * S:(par + 1) * S], in_=kscr[p, :, :]), 16))
        P.emit("sync", lambda E, p=p, par=par, ss=sStgp[par]: ss.inc(
            E.dma_start(out=vst[:, par * NKT * 130:(par + 1) * NKT * 130],
                        in_=vscr[p, :, :]), 16))

    stage_pair(0)
    stage_pair(1)

    P.wait("vector", sEvV)
    P.wait("vector", sEvKQ)
    P.emit("vector", lambda E: E.memset(SAp[0][:], 0.0))
    P.emit("vector", lambda E: sCev.inc(E.memset(SAp[1][:], 0.0)))

    def slot_sets(s_):
        qs = []
        nk = 16 * s_ + 16
        for kt in range(min(nk, 16 * s_)):
            for h in range(2):
                qs.append((h, kt, 0, QT))
        for h in range(2):
            for kt in range(16 * s_, min(nk, FLASH_KT_LIMIT)):
                q0 = 32 * (kt - 16 * s_)
                qs.append((h, kt, q0, QT - q0))
        sets = []
        cur, bank_i, bank_used, bank_h = [], 0, 0, None
        for (h, kt, q0, n) in qs:
            if bank_used + n > QT or (bank_h is not None and bank_h != h):
                bank_i += 1
                bank_used = 0
                bank_h = None
            if bank_i == 3:
                sets.append(cur)
                cur, bank_i, bank_used, bank_h = [], 0, 0, None
            cur.append((h, kt, q0, n, bank_i * QT + bank_used))
            bank_used += n
            bank_h = h
        if cur:
            sets.append(cur)
        return sets

    g_set = [0]
    mask_cnt = [0]
    mask_hist = {-1: 0, 0: 0}
    lw_n = [0]

    pending_av = [None]
    for p in range(min(NPAIR, FLASH_PAIRS)):
        par = p % 2
        P.wait("tensor", sStgp[p % 2], 32 * (p // 2 + 1))
        if p == 0:
            P.wait("tensor", sMsk, msk_done)
            P.wait("scalar", sMsk, msk_done)
            P.wait("tensor", sCev)
        for s_ in range(min(NSLOT, FLASH_SLOTS)):
            sets = slot_sets(s_)
            first_av = {0: True, 1: True}
            cev_target = sCev.n
            n_sets = len(sets)

            def emit_av(g, st_, buf, av_meta,
                        first_av=first_av, cev_target=cev_target,
                        par=par, s_=s_, p=p):
                if mask_hist[g] > mask_hist.get(g - 1, 0):
                    P.wait("tensor", sMa, mask_hist[g])
                else:
                    P.wait("tensor", sExp, g + 1)
                if av_meta["first"]:
                    P.wait("tensor", sCev, cev_target)
                for qi, (h, kt, q0, n, off) in enumerate(st_):
                    stt = first_av[h]
                    first_av[h] = False
                    last_q = qi == len(st_) - 1
                    stp = kt == 16 * s_ + 15
                    def avmm(E, h=h, kt=kt, q0=q0, n=n, off=off, buf=buf,
                             par=par, stt=stt, stp=stp):
                        return E.matmul(
                            CHp[h][0:65, q0:QT],
                            vst[:, par * NKT * 130 + kt * 130 + h * 65:
                                par * NKT * 130 + kt * 130 + h * 65 + 65],
                            pT[:, buf * SET_COLS + off: buf * SET_COLS + off + n],
                            start=stt, stop=stp)
                    if last_q:
                        P.emit("tensor", lambda E, f=avmm: sAV.inc(f(E)))
                    else:
                        P.emit("tensor", lambda E, f=avmm: f(E))

            for si, st_ in enumerate(sets):
                if g_set[0] >= FLASH_SETS:
                    raise _StopBuild(nc, P)
                g = g_set[0]
                buf = g % 2
                if g >= 2:
                    P.wait("tensor", sExp, g - 1)
                for qi, (h, kt, q0, n, off) in enumerate(st_):
                    last = qi == len(st_) - 1
                    def smm(E, h=h, kt=kt, q0=q0, n=n, off=off, buf=buf,
                            p=p, s_=s_, par=par):
                        return E.matmul(
                            SAp[buf][:, off:off + n],
                            kst[h * 64:(h + 1) * 64,
                                par * S + kt * 128: par * S + (kt + 1) * 128],
                            qT[h * 64:(h + 1) * 64,
                               p * TOK + s_ * QT + q0: p * TOK + (s_ + 1) * QT],
                            start=True, stop=True)
                    if last:
                        P.emit("tensor", lambda E, f=smm: sS.inc(f(E)))
                    else:
                        P.emit("tensor", lambda E, f=smm: f(E))
                if pending_av[0] is not None:
                    pending_av[0]()
                P.wait("scalar", sS, g + 1)
                if g >= 2:
                    P.wait("scalar", sAV, g - 1)
                    P.wait("scalar", sMa, mask_hist[g - 2])
                segs = {}
                for (h_, kt_, q0_, n_, off_) in st_:
                    bnk = off_ // QT
                    lo, hi = segs.get(bnk, (off_, off_))
                    segs[bnk] = (min(lo, off_), max(hi, off_ + n_))
                seg_list = []
                for lo, hi in sorted(segs.values()):
                    if seg_list and lo - seg_list[-1][1] < 600:
                        seg_list[-1] = (seg_list[-1][0], hi)
                    else:
                        seg_list.append((lo, hi))
                for gi, (lo, hi) in enumerate(seg_list):
                    last_seg = gi == len(seg_list) - 1
                    def expop(E, buf=buf, lo=lo, hi=hi):
                        return E.activation(
                            out=pT[:, buf * SET_COLS + lo: buf * SET_COLS + hi],
                            in_=SAp[buf][:, lo:hi], func=AF.Exp, scale=SMSCALE)
                    if last_seg:
                        P.emit("scalar", lambda E, f=expop: sExp.inc(f(E)))
                    else:
                        P.emit("scalar", lambda E, f=expop: f(E))
                P.wait("vector", sExp, g + 1)
                for (h, kt, q0, n, off) in st_:
                    b_ = kt - 16 * s_
                    if b_ >= 0:
                        P.emit("vector", lambda E, buf=buf, off=off, n=n, b_=b_, q0=q0:
                               sMa.inc(E.tensor_tensor(
                                   pT[:, buf * SET_COLS + off: buf * SET_COLS + off + n],
                                   pT[:, buf * SET_COLS + off: buf * SET_COLS + off + n],
                                   masks[:, b_ * QT + q0:(b_ + 1) * QT],
                                   AluOpType.mult)))
                        mask_cnt[0] += 1
                mask_hist[g] = mask_cnt[0]
                av_meta = {"first": first_av[0]}
                pending_av[0] = (lambda g=g, st_=st_, buf=buf, av_meta=av_meta,
                                 f=emit_av: f(g, st_, buf, av_meta))
                g_set[0] += 1
            if not FLASH_EVAC:
                raise _StopBuild(nc, P)
            P.wait("vector", sAV, g_set[0])
            n_ev = lw_n[0]
            cpar = n_ev % 2
            if n_ev >= 2:
                P.wait("vector", sCWp[cpar], 16 * (n_ev // 2))
                P.wait("vector", sLWp[cpar], 32 * (n_ev // 2))
            r = p * NSLOT + s_
            P.emit("vector", lambda E, cpar=cpar: sCev.inc(E.tensor_copy(
                ctmp[0:64, cpar * QT:(cpar + 1) * QT], CHp[0][0:64, :])))
            P.emit("vector", lambda E, cpar=cpar: sCev.inc(E.tensor_copy(
                ctmp[64:128, cpar * QT:(cpar + 1) * QT], CHp[1][0:64, :])))
            P.emit("vector", lambda E, cpar=cpar: sCev.inc(E.tensor_copy(
                lrow[0:1, cpar * QT:(cpar + 1) * QT], CHp[0][64:65, :])))
            P.emit("vector", lambda E, cpar=cpar: sCev.inc(E.tensor_copy(
                lrow[32:33, cpar * QT:(cpar + 1) * QT], CHp[1][64:65, :])))
            P.wait("sync", sCev)
            P.emit("sync", lambda E, p=p, s_=s_, cpar=cpar, sc=sCWp[cpar]:
                   sc.inc(E.dma_start(
                       out=ctxscr[p, :, s_ * QT:(s_ + 1) * QT],
                       in_=ctmp[:, cpar * QT:(cpar + 1) * QT]), 16))
            P.emit("sync", lambda E, r=r, cpar=cpar, sl=sLWp[cpar]: sl.inc(E.dma_start(
                out=lscr[0, r, :], in_=lrow[0:1, cpar * QT:(cpar + 1) * QT]), 16))
            P.emit("sync", lambda E, r=r, cpar=cpar, sl=sLWp[cpar]: sl.inc(E.dma_start(
                out=lscr[1, r, :], in_=lrow[32:33, cpar * QT:(cpar + 1) * QT]), 16))
            lw_n[0] += 1
        pair_end_av.append(g_set[0])
        if p + 2 < NPAIR:
            stage_pair(p + 2)

    if pending_av[0] is not None:
        pending_av[0]()
    flash_cev = sCev.n
    flash_lw = (sLWp[0].n, sLWp[1].n)
    flash_cw = (sCWp[0].n, sCWp[1].n)

    ckpt(3)
    fls.close()
    eraQ.close()

    eraU = ExitStack()
    u1 = eraU.enter_context(nc.sbuf_tensor("u1", [128, NC * TOK], F32))

    p3s = ExitStack()
    sb3 = lambda nm, shp, dt: p3s.enter_context(nc.sbuf_tensor(nm, shp, dt))
    ps3 = lambda nm, shp, dt: p3s.enter_context(nc.psum_tensor(nm, shp, dt))
    ltile = sb3("ltile", [32, QT], F32)
    ltile2 = sb3("ltile2", [32, QT], F32)
    rl33 = sb3("rl33", [33, 2 * 4096], F32)
    LBsb = sb3("LBsb", [128, 4096], F32)
    cstage = sb3("cstage", [128, 2 * NC * QT], BF16)
    xot = sb3("xot", [128, 4 * QT], F32)
    u1t = sb3("u1t", [128, 2 * QT], F32)
    LBp = ps3("LBp", [128, 2048], F32)
    WOp = [ps3("wop0", [128, QT], F32), ps3("wop1", [128, QT], F32)]

    if debug:
        P.wait("gpsimd", sCWp[0], flash_cw[0])
        P.wait("gpsimd", sCWp[1], flash_cw[1])
        P.wait("gpsimd", sLWp[0], flash_lw[0])
        P.wait("gpsimd", sLWp[1], flash_lw[1])
        for p in range(NPAIR):
            P.emit("gpsimd", lambda E, p=p: sOutG.inc(E.dma_start(
                out=dbg["dbg_ctx"][p * 128:(p + 1) * 128, :],
                in_=ctxscr[p, :, :]), 16))
        P.emit("gpsimd", lambda E: sOutG.inc(E.dma_start(
            out=dbg["dbg_l"][:], in_=lscr[:, :, :]), 16))

    P.wait("sync", sLWp[0], flash_lw[0])
    P.wait("sync", sLWp[1], flash_lw[1])
    if debug:
        P.wait("sync", sOutG)
    P.emit("sync", lambda E: sLd.inc(E.dma_start(out=ltile[:], in_=lscr[0, :, :]), 16))
    P.emit("sync", lambda E: sLd.inc(E.dma_start(out=ltile2[:], in_=lscr[1, :, :]), 16))
    P.wait("vector", sLd)
    P.emit("vector", lambda E: sRe.inc(E.reciprocal(ltile[:], ltile[:])))
    P.emit("vector", lambda E: sRe.inc(E.reciprocal(ltile2[:], ltile2[:])))
    P.emit("vector", lambda E: sRe.inc(E.memset(rl33[:], 0.0)))
    P.wait("sync", sRe)
    P.emit("sync", lambda E: sLdW.inc(E.dma_start(out=lscr[0], in_=ltile[:]), 16))
    P.emit("sync", lambda E: sLdW.inc(E.dma_start(out=lscr[1], in_=ltile2[:]), 16))
    recip_w = sLdW.n

    P.wait("sync", sLdW, recip_w)
    P.wait("sync", sCWp[0], flash_cw[0])
    P.wait("sync", sCWp[1], flash_cw[1])
    P.wait("tensor", sW, w_ready)
    P.wait("scalar", sW, w_ready)
    wo_n = [0]
    ld_n = [0]
    xo_n = [0]
    for k in range(2):
        P.emit("sync", lambda E, k=k, sx=sXOp[k]: sx.inc(E.dma_start(
            out=xot[:, k * QT:(k + 1) * QT],
            in_=xTo[k * 128:(k + 1) * 128, 0:QT]), 16))
        xo_n[0] += 1
    for tt in range(TOK // QT):
        par = tt % 2
        if tt >= 2:
            P.wait("sync", sWoPE, 8 * (tt - 1))
            P.wait("sync", sNrmM, 8 * (tt - 1))
        P.emit("sync", lambda E, tt=tt, par=par, sc=sCSp[par]:
               sc.inc(E.dma_start(
                   out=cstage.rearrange("p (b cc q) -> p b cc q",
                                        b=2, cc=NC)[:, par],
                   in_=ctxscr[:, :, tt * QT:(tt + 1) * QT]
                       .rearrange("cc p q -> p cc q")), 16))
        P.emit("sync", lambda E, tt=tt, par=par, sl=sLdp[par]: sl.inc(E.dma_start(
            out=rl33[0:1, par * 4096:(par + 1) * 4096]
                .rearrange("p (r q) -> p r q", r=NC),
            in_=lscr[0].rearrange("(c s) q -> c s q", s=NSLOT)[:, tt]), 16))
        P.emit("sync", lambda E, tt=tt, par=par, sl=sLdp[par]: sl.inc(E.dma_start(
            out=rl33[32:33, par * 4096:(par + 1) * 4096]
                .rearrange("p (r q) -> p r q", r=NC),
            in_=lscr[1].rearrange("(c s) q -> c s q", s=NSLOT)[:, tt]), 16))
        P.wait("tensor", sLdp[par])
        P.wait("tensor", sRe, 3)
        for w in range(2):
            P.wait("tensor", sNrmC, 2 * tt + w)
            for k in range(4):
                cc = w * 4 + k
                P.emit("tensor", lambda E, cc=cc, k=k, par=par: sNrmM.inc(E.matmul(
                    LBp[:, k * QT:(k + 1) * QT], sel33[:],
                    rl33[0:33, par * 4096 + cc * QT: par * 4096 + (cc + 1) * QT],
                    start=True, stop=True)))
            P.wait("vector", sNrmM)
            P.emit("vector", lambda E, w=w: sNrmC.inc(E.tensor_copy(
                LBsb[:, w * 2048:(w + 1) * 2048], LBp[:])))
        P.wait("vector", sNrmC, 2 * (tt + 1))
        P.wait("vector", sCSp[par], 16 * (tt // 2 + 1))
        P.emit("vector", lambda E, par=par: sNrmX.inc(E.tensor_tensor(
            cstage[:, par * NC * QT:(par + 1) * NC * QT],
            cstage[:, par * NC * QT:(par + 1) * NC * QT],
            LBsb[:], AluOpType.mult)))
        stage_done = sNrmX.n
        for c in range(NC):
            i = wo_n[0]
            bank = WOp[i % 2]
            nxt = i + 2
            if nxt < 32:
                ntt, ncz = nxt // NC, nxt % NC
                if xo_n[0] >= 4:
                    P.wait("sync", sU1, xo_n[0] - 3)
                P.emit("sync", lambda E, ncz=ncz, ntt=ntt, sl=(xo_n[0] % 4),
                       sx=sXOp[xo_n[0] % 4]:
                       sx.inc(E.dma_start(
                           out=xot[:, sl * QT:(sl + 1) * QT],
                           in_=xTo[ncz * 128:(ncz + 1) * 128,
                                   ntt * QT:(ntt + 1) * QT]), 16))
                xo_n[0] += 1
            if i >= 2:
                P.wait("tensor", sWoEv, i - 1)
            P.wait("tensor", sNrmX, stage_done)
            for cc in range(NC):
                lastmm = cc == NC - 1
                def wmm(E, o=bank[:], l=woS[:, cc * D + c * 128: cc * D + (c + 1) * 128],
                        r=cstage[:, (par * NC + cc) * QT:(par * NC + cc + 1) * QT],
                        st=(cc == 0), sp=lastmm):
                    return E.matmul(o, l, r, start=st, stop=sp)
                if lastmm:
                    P.emit("tensor", lambda E, f=wmm: sWoPE.inc(f(E)))
                else:
                    P.emit("tensor", lambda E, f=wmm: f(E))
            wo_n[0] += 1
            P.wait("scalar", sWoPE)
            if i >= 2:
                P.wait("scalar", sU1, i - 1)
            P.emit("scalar", lambda E, bank=bank, c=c, i=i: sWoEv.inc(E.activation(
                out=u1t[:, (i % 2) * QT:(i % 2 + 1) * QT], in_=bank[:],
                func=AF.Identity, bias=bocS[:, c:c + 1])))
            P.wait("vector", sWoEv)
            P.wait("vector", sXOp[i % 4], 16 * (i // 4 + 1))
            P.emit("vector", lambda E, c=c, tt=tt, i=i: sU1.inc(E.tensor_tensor(
                u1[:, c * TOK + tt * QT: c * TOK + (tt + 1) * QT],
                u1t[:, (i % 2) * QT:(i % 2 + 1) * QT],
                xot[:, (i % 4) * QT:(i % 4 + 1) * QT],
                AluOpType.add)))
    u1_done = sU1.n

    if debug:
        P.wait("sync", sU1, u1_done)
        for c in range(NC):
            P.emit("sync", lambda E, c=c: sOut.inc(E.dma_start(
                out=dbg["dbg_u1"][c * 128:(c + 1) * 128, :],
                in_=u1[:, c * TOK:(c + 1) * TOK]), 16))
        P.wait("vector", sOut)

    ckpt(4)
    p3s.close()

    ln_id = [0]

    def emit_ln(src, dst_bf16, out_dram, ready_sem, ready_val, dbg_out=None):
        ln_id[0] += 1
        tag = f"_{ln_id[0]}"
        ln = ExitStack()
        sbl = lambda nm, shp, dt: ln.enter_context(nc.sbuf_tensor(nm + tag, shp, dt))
        psl = lambda nm, shp, dt: ln.enter_context(nc.psum_tensor(nm + tag, shp, dt))
        xsq = sbl("xsq", [128, 2 * TOK], F32)
        rm = sbl("rm", [1, TOK], F32)
        rv = sbl("rv", [1, TOK], F32)
        ra = sbl("ra", [1, TOK], F32)
        bcA = sbl("bcA", [128, TOK], F32)
        bcB = sbl("bcB", [128, TOK], F32)
        stats = psl("stats", [128, TOK], F32)
        bcp = psl("bcp", [128, TOK], F32)
        a0 = sLnA.n
        p0_ = sLnP.n
        P.wait("tensor", ready_sem, ready_val)
        P.wait("scalar", ready_sem, ready_val)
        for c in range(NC):
            xpar = c % 2
            if c >= 2:
                P.wait("scalar", sLnP, p0_ + c - 1)
            P.emit("scalar", lambda E, c=c, xpar=xpar: sLnA.inc(E.activation(
                out=xsq[:, xpar * TOK:(xpar + 1) * TOK],
                in_=src[:, c * TOK:(c + 1) * TOK], func=AF.Square)))
            for nt in range(TOK // QT):
                P.emit("tensor", lambda E, c=c, nt=nt: E.matmul(
                    stats[0:1, nt * QT:(nt + 1) * QT], ones128[:],
                    src[:, c * TOK + nt * QT: c * TOK + (nt + 1) * QT],
                    start=(c == 0), stop=(c == NC - 1)))
            P.wait("tensor", sLnA, a0 + c + 1)
            for nt in range(TOK // QT):
                lastmm = nt == TOK // QT - 1
                def qmm(E, c=c, nt=nt, xpar=xpar):
                    return E.matmul(
                        stats[32:33, nt * QT:(nt + 1) * QT], ones128[:],
                        xsq[:, xpar * TOK + nt * QT: xpar * TOK + (nt + 1) * QT],
                        start=(c == 0), stop=(c == NC - 1))
                if lastmm:
                    P.emit("tensor", lambda E, f=qmm: sLnP.inc(f(E)))
                else:
                    P.emit("tensor", lambda E, f=qmm: f(E))
        P.wait("vector", sLnP, p0_ + NC)
        P.emit("vector", lambda E: sLnR.inc(E.tensor_scalar(
            rm[:], stats[0:1, :], 1.0 / D, None, AluOpType.mult)))
        P.emit("vector", lambda E: sLnR.inc(E.tensor_scalar(
            rv[:], stats[32:33, :], 1.0 / D, None, AluOpType.mult)))
        P.wait("vector", sLnR)
        P.emit("vector", lambda E: sLnR.inc(E.tensor_tensor(
            ra[:], rm[:], rm[:], AluOpType.mult)))
        P.wait("vector", sLnR)
        P.emit("vector", lambda E: sLnR.inc(E.tensor_tensor(
            rv[:], rv[:], ra[:], AluOpType.subtract)))
        P.wait("scalar", sLnR)
        P.emit("scalar", lambda E: sLnR.inc(E.activation(
            out=ra[:], in_=rv[:], func=AF.Ln, bias=epsc[0:1, 0:1])))
        P.wait("scalar", sLnR)
        P.emit("scalar", lambda E: sLnR.inc(E.activation(
            out=ra[:], in_=ra[:], func=AF.Exp, scale=-0.5)))
        P.wait("vector", sLnR)
        P.emit("vector", lambda E: sLnR.inc(E.tensor_tensor(
            rm[:], rm[:], ra[:], AluOpType.mult)))
        P.wait("tensor", sLnR)
        for nt in range(TOK // QT):
            last = nt == TOK // QT - 1
            def bmm1(E, nt=nt):
                return E.matmul(bcp[:, nt * QT:(nt + 1) * QT], ones1[:],
                                ra[0:1, nt * QT:(nt + 1) * QT], start=True, stop=True)
            if last:
                P.emit("tensor", lambda E, f=bmm1: sLnB.inc(f(E)))
            else:
                P.emit("tensor", lambda E, f=bmm1: f(E))
        P.wait("vector", sLnB)
        P.emit("vector", lambda E: sLnB.inc(E.tensor_copy(bcA[:], bcp[:])))
        P.wait("tensor", sLnB)
        for nt in range(TOK // QT):
            last = nt == TOK // QT - 1
            def bmm2(E, nt=nt):
                return E.matmul(bcp[:, nt * QT:(nt + 1) * QT], ones1[:],
                                rm[0:1, nt * QT:(nt + 1) * QT], start=True, stop=True)
            if last:
                P.emit("tensor", lambda E, f=bmm2: sLnB.inc(f(E)))
            else:
                P.emit("tensor", lambda E, f=bmm2: f(E))
        P.wait("vector", sLnB)
        P.emit("vector", lambda E: sLnB.inc(E.tensor_copy(bcB[:], bcp[:])))
        P.wait("vector", sLnB)
        for c in range(NC):
            P.emit("vector", lambda E, c=c: sApp.inc(E.tensor_tensor(
                src[:, c * TOK:(c + 1) * TOK], src[:, c * TOK:(c + 1) * TOK],
                bcA[:], AluOpType.mult)))
            P.wait("vector", sApp)
            P.emit("vector", lambda E, c=c: sApp.inc(E.tensor_tensor(
                src[:, c * TOK:(c + 1) * TOK], src[:, c * TOK:(c + 1) * TOK],
                bcB[:], AluOpType.subtract)))
            P.wait("vector", sApp)
            if dst_bf16 is not None:
                P.emit("vector", lambda E, c=c: sApp.inc(E.tensor_copy(
                    dst_bf16[:, c * TOK:(c + 1) * TOK],
                    src[:, c * TOK:(c + 1) * TOK])))
            if out_dram is not None:
                P.wait("sync", sApp)
                P.emit("sync", lambda E, c=c: sOut.inc(E.dma_start(
                    out=out_dram[c * 128:(c + 1) * 128, :],
                    in_=src[:, c * TOK:(c + 1) * TOK]), 16))
            if dbg_out is not None:
                P.wait("sync", sApp)
                P.emit("sync", lambda E, c=c: sOut.inc(E.dma_start(
                    out=dbg_out[c * 128:(c + 1) * 128, :],
                    in_=src[:, c * TOK:(c + 1) * TOK]), 16))
        ln.close()
        return sApp.n

    eraY = ExitStack()
    y1bf = eraY.enter_context(nc.sbuf_tensor("y1bf", [128, NC * TOK], BF16))

    eraW = ExitStack()
    w1s = eraW.enter_context(nc.sbuf_tensor("w1s", [128, 3 * 1024], BF16))
    w1_ld = [0]
    w2_ld = [0]

    def load_w1(j):
        P.emit("sync", lambda E, hc=(j % HC), sl=(j % 3), sw=sFW1q[j % 3]:
               sw.inc(E.dma_start(
                   out=w1s[:, sl * 1024:(sl + 1) * 1024], in_=w1bf[hc]), 16))
        w1_ld[0] += 1

    P.wait("sync", sWB, wb_done)
    load_w1(0)
    load_w1(1)

    ln1_done = emit_ln(u1, y1bf, None, sU1, u1_done,
                       dbg_out=dbg.get("dbg_y1") if debug else None)
    if debug:
        P.wait("vector", sOut)
    y1 = u1

    ckpt(5)
    ffs = ExitStack()
    sbF = lambda nm, shp, dt: ffs.enter_context(nc.sbuf_tensor(nm, shp, dt))
    psF = lambda nm, shp, dt: ffs.enter_context(nc.psum_tensor(nm, shp, dt))
    hblk = sbF("hblk", [128, HC * QT], BF16)
    w2s = sbF("w2s", [128, 2 * 4096], BF16)
    zgbf = sbF("zgbf", [128, 4 * ER], BF16)
    zgT = sbF("zgT", [24, QT], BF16)
    zg2bf = sbF("zg2bf", [128, 4 * ER], BF16)
    zg2T = sbF("zg2T", [24, QT], BF16)
    ge = sbF("ge", [128, 4], F32)
    gn = sbF("gn", [128, 4], F32)
    gneg = sbF("gneg", [128, 1], F32)
    u2t = sbF("u2t", [128, 2 * QT], F32)
    Z1 = psF("Z1", [128, QT], F32)
    Z2 = psF("Z2", [128, QT], F32)
    TPp = psF("TPp", [ER, 128], BF16)
    B1p = [psF("b1p0", [128, QT], F32), psF("b1p1", [128, QT], F32)]
    M2p = [psF("m2p0", [128, QT], F32), psF("m2p1", [128, QT], F32)]

    b1_n = [0]
    b2_n = [0]

    def load_w2(j):
        P.emit("sync", lambda E, dc=(j % NC), par=(j % 2), sw=sFW2p[j % 2]:
               sw.inc(E.dma_start(
                   out=w2s[:, par * 4096:(par + 1) * 4096], in_=w2bf[dc]), 16))
        w2_ld[0] += 1

    def gates_and_zg(Z, dst_zg, dst_zgT, sub, zsem_val):
        P.wait("vector", sFz, zsem_val)
        P.emit("vector", lambda E: sFg.inc(E.reduce_max(
            gneg[:], Z[:, 24:27], AX.X, negate=True)))
        P.wait("scalar", sFg)
        P.emit("scalar", lambda E: sFg.inc(E.activation(
            out=ge[:, 0:3], in_=Z[:, 24:27],
            func=AF.Exp, bias=gneg[:, 0:1])))
        P.wait("vector", sFg)
        P.emit("vector", lambda E: sFg.inc(E.reduce_sum(
            gn[:, 0:1], ge[:, 0:3], AX.X)))
        P.wait("vector", sFg)
        P.emit("vector", lambda E: sFg.inc(E.reciprocal(gn[:, 1:2], gn[:, 0:1])))
        P.wait("vector", sFg)
        P.emit("vector", lambda E: sFg.inc(E.tensor_scalar(
            ge[:, 0:3], ge[:, 0:3], gn[:, 1:2], None, AluOpType.mult)))
        P.wait("vector", sFg)
        for e_ in range(3):
            last = e_ == 2
            def zgop(E, e_=e_, sub=sub):
                return E.tensor_scalar(
                    dst_zg[:, sub * ER + e_ * 8: sub * ER + e_ * 8 + 8],
                    Z[:, e_ * 8: e_ * 8 + 8],
                    ge[:, e_:e_ + 1], LSCALE, AluOpType.mult, AluOpType.mult)
            if last:
                P.emit("vector", lambda E, f=zgop: sFzg.inc(f(E)))
            else:
                P.emit("vector", lambda E, f=zgop: f(E))
        P.wait("tensor", sFzg)
        P.emit("tensor", lambda E, sub=sub: sFt.inc(E.transpose(
            TPp[:], dst_zg[:, sub * ER:(sub + 1) * ER], ident[:])))
        P.wait("vector", sFt)
        P.emit("vector", lambda E, sub=sub: sFt.inc(E.tensor_copy(
            dst_zgT[:, sub * 128:(sub + 1) * 128], TPp[:])))

    P.wait("sync", sApp, ln1_done)
    load_w2(0)
    P.wait("tensor", sApp, ln1_done)

    zg0 = sFzg.n
    for tb in range(TOK // QT):
        for sub in range(4):
            P.wait("tensor", sFzg, zg0 + 8 * tb + sub)
            for c in range(NC):
                lastmm = c == NC - 1
                def zmm(E, c=c, sub=sub, tb=tb):
                    return E.matmul(
                        Z1[:, 0:28],
                        y1bf[:, c * TOK + tb * QT + sub * 128:
                             c * TOK + tb * QT + (sub + 1) * 128],
                        a1r1S[:, c * 32: c * 32 + 28],
                        start=(c == 0), stop=(c == NC - 1))
                if lastmm:
                    P.emit("tensor", lambda E, f=zmm: sFz.inc(f(E)))
                else:
                    P.emit("tensor", lambda E, f=zmm: f(E))
            gates_and_zg(Z1, zgbf, zgT, sub, sFz.n)
        for hc in range(HC):
            i = b1_n[0]
            par = i % 3
            if i + 2 < 4 * HC:
                P.wait("sync", sFb1, i)
                load_w1(i + 2)
            bank = B1p[i % 2]
            if i >= 2:
                P.wait("tensor", sFh, i - 1)
            P.wait("tensor", sFW1q[par], 16 * (i // 3 + 1))
            P.wait("tensor", sFt)
            for c in range(NC):
                P.emit("tensor", lambda E, c=c, par=par, bank=bank, tb=tb:
                       E.matmul(bank[:],
                                w1s[:, par * 1024 + c * 128: par * 1024 + (c + 1) * 128],
                                y1bf[:, c * TOK + tb * QT: c * TOK + (tb + 1) * QT],
                                start=(c == 0), stop=False))
            P.emit("tensor", lambda E, hc=hc, bank=bank: sFb1.inc(E.matmul(
                bank[:], b1S[:, hc * 128:(hc + 1) * 128], zgT[:],
                start=False, stop=True)))
            b1_n[0] += 1
            P.wait("scalar", sFb1)
            P.emit("scalar", lambda E, hc=hc, bank=bank: sFh.inc(E.activation(
                out=hblk[:, hc * QT:(hc + 1) * QT], in_=bank[:],
                func=AF.Relu, bias=b1cS[:, hc:hc + 1])))
        h_done = sFh.n

        if debug and tb == 0:
            P.wait("gpsimd", sFh, h_done)
            for hc in range(HC):
                P.emit("gpsimd", lambda E, hc=hc: sOutG.inc(E.dma_start(
                    out=dbg["dbg_h"][hc * 128:(hc + 1) * 128, :],
                    in_=hblk[:, hc * QT:(hc + 1) * QT]), 16))
            P.wait("scalar", sOutG)

        P.wait("tensor", sFh, h_done)
        for sub in range(4):
            P.wait("tensor", sFzg, zg0 + 8 * tb + 4 + sub)
            for hc in range(HC):
                lastmm = hc == HC - 1
                def z2mm(E, hc=hc, sub=sub):
                    return E.matmul(
                        Z2[:, 0:28],
                        hblk[:, hc * QT + sub * 128: hc * QT + (sub + 1) * 128],
                        a2r2S[:, hc * 32: hc * 32 + 28],
                        start=(hc == 0), stop=(hc == HC - 1))
                if lastmm:
                    P.emit("tensor", lambda E, f=z2mm: sFz.inc(f(E)))
                else:
                    P.emit("tensor", lambda E, f=z2mm: f(E))
            gates_and_zg(Z2, zg2bf, zg2T, sub, sFz.n)
        for dc in range(NC):
            i = b2_n[0]
            par = i % 2
            if i + 1 < 4 * NC:
                P.wait("sync", sFb2, i)
                load_w2(i + 1)
            bank = M2p[i % 2]
            if i >= 2:
                P.wait("tensor", sFu, i - 1)
            P.wait("tensor", sFW2p[par], 16 * (i // 2 + 1))
            P.wait("tensor", sFt)
            for hc in range(HC):
                P.emit("tensor", lambda E, hc=hc, par=par, bank=bank:
                       E.matmul(bank[:],
                                w2s[:, par * 4096 + hc * 128: par * 4096 + (hc + 1) * 128],
                                hblk[:, hc * QT:(hc + 1) * QT],
                                start=(hc == 0), stop=False))
            P.emit("tensor", lambda E, dc=dc, bank=bank: sFb2.inc(E.matmul(
                bank[:], b2S[:, dc * 128:(dc + 1) * 128], zg2T[:],
                start=False, stop=True)))
            b2_n[0] += 1
            P.wait("scalar", sFb2)
            if i >= 2:
                P.wait("scalar", sFu2, i - 1)
            P.emit("scalar", lambda E, dc=dc, bank=bank, i=i: sFu.inc(E.activation(
                out=u2t[:, (i % 2) * QT:(i % 2 + 1) * QT], in_=bank[:],
                func=AF.Identity, bias=b2cS[:, dc:dc + 1])))
            P.wait("vector", sFu)
            P.emit("vector", lambda E, dc=dc, tb=tb, i=i: sFu2.inc(E.tensor_tensor(
                y1[:, dc * TOK + tb * QT: dc * TOK + (tb + 1) * QT],
                y1[:, dc * TOK + tb * QT: dc * TOK + (tb + 1) * QT],
                u2t[:, (i % 2) * QT:(i % 2 + 1) * QT], AluOpType.add)))

    ffn_done = sFu2.n

    if debug:
        P.wait("sync", sFu2, ffn_done)
        for c in range(NC):
            P.emit("sync", lambda E, c=c: sOut.inc(E.dma_start(
                out=dbg["dbg_u2"][c * 128:(c + 1) * 128, :],
                in_=y1[:, c * TOK:(c + 1) * TOK]), 16))
        P.wait("vector", sOut)

    ckpt(6)
    ffs.close()
    eraW.close()
    eraY.close()
    emit_ln(y1, None, yout, sFu2, ffn_done)
    eraU.close()

    P.wait("sync", sOut, sOut.n)
    if debug:
        P.wait("gpsimd", sOutG, sOutG.n)
    ckpt(99)

    _emit_block(nc, P)
    return nc


def _emit_block(nc, P):
    with nc.Block() as blk:
        blk.tensor(lambda E: [f(E) for f in P.ops["tensor"]])
        blk.vector(lambda E: [f(E) for f in P.ops["vector"]])
        blk.scalar(lambda E: [f(E) for f in P.ops["scalar"]])
        blk.gpsimd(lambda E: [f(E) for f in P.ops["gpsimd"]])
        blk.sync(lambda E: [f(E) for f in P.ops["sync"]])


_NC_CACHE = {}


def _get_nc(debug=False):
    key = bool(debug)
    if key not in _NC_CACHE:
        _NC_CACHE[key] = build_nc(debug=key)
    return _NC_CACHE[key]


def _stage(inputs, debug=False):
    x = np.asarray(inputs["L0_attn_x"], dtype=np.float32)
    Wq = np.asarray(inputs["L0_attn_Wq"], np.float32)
    Wk = np.asarray(inputs["L0_attn_Wk"], np.float32)
    Wv = np.asarray(inputs["L0_attn_Wv"], np.float32)
    Wo = np.asarray(inputs["L0_attn_Wo"], np.float32)
    bo = np.asarray(inputs["L0_attn_bo"], np.float32)
    W1 = np.asarray(inputs["L0_ffn_W1"], np.float32)
    b1 = np.asarray(inputs["L0_ffn_b1"], np.float32)
    A1 = np.asarray(inputs["L0_ffn_A1"], np.float32)
    B1 = np.asarray(inputs["L0_ffn_B1"], np.float32)
    R1 = np.asarray(inputs["L0_ffn_R1"], np.float32)
    W2 = np.asarray(inputs["L0_ffn_W2"], np.float32)
    b2 = np.asarray(inputs["L0_ffn_b2"], np.float32)
    A2 = np.asarray(inputs["L0_ffn_A2"], np.float32)
    B2 = np.asarray(inputs["L0_ffn_B2"], np.float32)
    R2 = np.asarray(inputs["L0_ffn_R2"], np.float32)

    C = np.ascontiguousarray
    shared = {
        "wqT": C(Wq.T), "wkT": C(Wk.T), "wvT": C(Wv.T), "woT": C(Wo.T),
        "boc": C(bo.reshape(NC, 128).T),
        "w1": C(W1), "b1c": C(b1.reshape(HC, 128).T),
        "w2": C(W2), "b2c": C(b2.reshape(NC, 128).T),
        "a1r1": C(np.concatenate(
            [A1.transpose(1, 0, 2).reshape(D, ER), R1,
             np.zeros((D, 5), np.float32)], axis=1)),
        "b1cat": C(B1.reshape(ER, H)),
        "a2r2": C(np.concatenate(
            [A2.transpose(1, 0, 2).reshape(H, ER), R2,
             np.zeros((H, 5), np.float32)], axis=1)),
        "b2cat": C(B2.reshape(ER, D)),
    }
    xT = [C(x[b].T) for b in range(2)]
    in_maps = []
    for core in range(8):
        b, j = core // 4, core % 4
        m = dict(shared)
        m["xTb"] = xT[b]
        m["xTo"] = C(x[b, j::4, :].T)
        t = (128.0 * np.arange(16, dtype=np.float32) - j)[None, :]
        m["thr"] = C(np.broadcast_to(t, (128, 16)))
        in_maps.append(m)
    return in_maps


def kernel(**inputs):
    nc = _get_nc(debug=DEBUG)
    in_maps = _stage(inputs, debug=DEBUG)
    res = run_bass_kernel_spmd(nc, in_maps, core_ids=list(range(8)))
    x = inputs["L0_attn_x"]
    out = np.empty((2, S, D), dtype=np.float32)
    for core in range(8):
        b, j = core // 4, core % 4
        out[b, j::4, :] = res.results[core]["y"].T
    return out


# version 25 (final)
# speedup vs baseline: 1.3220x; 1.0146x over previous
import numpy as np
import concourse.bass as bass
import concourse.mybir as mybir
from concourse.bass_utils import run_bass_kernel_spmd
from concourse.alu_op_type import AluOpType
from contextlib import ExitStack

F32 = mybir.dt.float32
BF16 = mybir.dt.bfloat16
AF = mybir.ActivationFunctionType
AX = mybir.AxisListType

D = 1024
S = 8192
NPAIR = 8
NC = 8
TOK = 2048
NSLOT = 4
QT = 512
NKT = 64
H = 4096
HC = 32
ER = 24
EPS = 1e-5
SMSCALE = 0.125
LSCALE = 0.125
SET_COLS = 1536

DEBUG = False
FLASH_PAIRS = NPAIR
FLASH_SLOTS = NSLOT
FLASH_SETS = 10**9
FLASH_EVAC = True
FLASH_KT_LIMIT = 10**9


class SemT:
    def __init__(self, h):
        self.h = h
        self.n = 0

    def inc(self, instr, k=1):
        instr.then_inc(self.h, k)
        self.n += k
        return instr


class _Dummy:
    def then_inc(self, *a, **k):
        return self


class _Proxy:
    def __getattr__(self, name):
        return lambda *a, **k: _Dummy()


_PROXY = _Proxy()


class Prog:
    ENGS = ("tensor", "vector", "scalar", "gpsimd", "sync")

    def __init__(self):
        self.ops = {e: [] for e in self.ENGS}
        self.last_wait = {}

    def emit(self, eng, fn):
        fn(_PROXY)
        self.ops[eng].append(fn)

    def wait(self, eng, sem, val=None):
        v = sem.n if val is None else val
        key = (eng, id(sem))
        if v > 0 and v > self.last_wait.get(key, 0):
            self.last_wait[key] = v
            self.ops[eng].append(lambda E, s=sem.h, v=v: E.wait_ge(s, v))


class _StopBuild(Exception):
    pass


def build_nc(debug=False, phases=99):
    try:
        return _build_nc_inner(debug, phases)
    except _StopBuild as e:
        nc, P = e.args
        _emit_block(nc, P)
        return nc


def _build_nc_inner(debug=False, phases=99):
    nc = bass.Bass("TRN2", target_bir_lowering=False, debug=False)
    P = Prog()

    def ckpt(n):
        if phases < n:
            raise _StopBuild(nc, P)

    xTb = nc.dram_tensor("xTb", [D, S], F32, kind="ExternalInput").ap()
    xTo = nc.dram_tensor("xTo", [D, TOK], F32, kind="ExternalInput").ap()
    wqT = nc.dram_tensor("wqT", [D, D], F32, kind="ExternalInput").ap()
    wkT = nc.dram_tensor("wkT", [D, D], F32, kind="ExternalInput").ap()
    wvT = nc.dram_tensor("wvT", [D, D], F32, kind="ExternalInput").ap()
    woT = nc.dram_tensor("woT", [D, D], F32, kind="ExternalInput").ap()
    boc = nc.dram_tensor("boc", [128, NC], F32, kind="ExternalInput").ap()
    w1d = nc.dram_tensor("w1", [D, H], F32, kind="ExternalInput").ap()
    b1c = nc.dram_tensor("b1c", [128, HC], F32, kind="ExternalInput").ap()
    w2d = nc.dram_tensor("w2", [H, D], F32, kind="ExternalInput").ap()
    b2c = nc.dram_tensor("b2c", [128, NC], F32, kind="ExternalInput").ap()
    a1r1 = nc.dram_tensor("a1r1", [D, 32], F32, kind="ExternalInput").ap()
    b1cat = nc.dram_tensor("b1cat", [ER, H], F32, kind="ExternalInput").ap()
    a2r2 = nc.dram_tensor("a2r2", [H, 32], F32, kind="ExternalInput").ap()
    b2cat = nc.dram_tensor("b2cat", [ER, D], F32, kind="ExternalInput").ap()
    thr = nc.dram_tensor("thr", [128, 16], F32, kind="ExternalInput").ap()
    yout = nc.dram_tensor("y", [D, TOK], F32, kind="ExternalOutput").ap()

    kscr = nc.dram_tensor("kscr", [NPAIR, 128, S], BF16, kind="Internal").ap()
    vscr = nc.dram_tensor("vscr", [NPAIR, 128, NKT * 130], BF16, kind="Internal").ap()
    lscr = nc.dram_tensor("lscr", [2, 32, QT], F32, kind="Internal").ap()
    ctxscr = nc.dram_tensor("ctxscr", [NPAIR, 128, TOK], BF16, kind="Internal").ap()
    w1bf = nc.dram_tensor("w1bf", [HC, 128, 1024], BF16, kind="Internal").ap()
    w2bf = nc.dram_tensor("w2bf", [NC, 128, 4096], BF16, kind="Internal").ap()

    dbg = {}
    if debug:
        for nm, shp in [("dbg_q", [D, TOK]), ("dbg_k", [D, S]),
                        ("dbg_v", [NPAIR * 128, NKT * 130]),
                        ("dbg_ctx", [D, TOK]), ("dbg_l", [2, 32, QT]),
                        ("dbg_u1", [D, TOK]), ("dbg_y1", [D, TOK]),
                        ("dbg_h", [H, QT]), ("dbg_u2", [D, TOK])]:
            dbg[nm] = nc.dram_tensor(nm, shp, F32, kind="ExternalOutput").ap()

    root = ExitStack()
    sbR = lambda nm, shp, dt: root.enter_context(nc.sbuf_tensor(nm, shp, dt))
    mksem = lambda nm: SemT(root.enter_context(nc.semaphore(nm)))

    lrow = sbR("lrow", [33, 2 * QT], F32)
    iot = sbR("iot", [128, QT], F32)
    thrS = sbR("thrS", [128, 16], F32)
    ident = sbR("ident", [128, 128], BF16)
    sel33 = sbR("sel33", [33, 128], F32)
    ones1 = sbR("ones1", [1, 128], F32)
    ones128 = sbR("ones128", [128, 1], F32)
    epsc = sbR("epsc", [1, 1], F32)
    bocS = sbR("bocS", [128, NC], F32)
    b1cS = sbR("b1cS", [128, HC], F32)
    b2cS = sbR("b2cS", [128, NC], F32)
    a1r1S = sbR("a1r1S", [128, NC * 32], BF16)
    a2r2S = sbR("a2r2S", [128, HC * 32], BF16)
    b1S = sbR("b1S", [ER, H], BF16)
    b2S = sbR("b2S", [ER, D], BF16)
    woS = sbR("woS", [128, NC * D], BF16)

    sW = mksem("sW"); sMsk = mksem("sMsk")
    sXp = [mksem("sX0"), mksem("sX1")]
    sXF = [mksem("sXF0"), mksem("sXF1")]; sXC = mksem("sXC")
    sTile = mksem("sTile"); sPE = mksem("sPE")
    sEvKQ = mksem("sEvKQ"); sEvV = mksem("sEvV")
    sKWp = [mksem("sKW0"), mksem("sKW1")]
    sVWp = [mksem("sVW0"), mksem("sVW1")]
    sStgp = [mksem("sStg0"), mksem("sStg1")]
    sS = mksem("sS"); sExp = mksem("sExp")
    sMa = mksem("sMa"); sAV = mksem("sAV"); sCev = mksem("sCev")
    sLWp = [mksem("sLW0"), mksem("sLW1")]
    sLd = mksem("sLd"); sLdW = mksem("sLdW"); sRe = mksem("sRe"); sNrm = mksem("sNrm")
    sLdp = [mksem("sLda"), mksem("sLdb")]
    sNrmM = mksem("sNrmM"); sNrmC = mksem("sNrmC"); sNrmX = mksem("sNrmX")
    sWoPE = mksem("sWoPE"); sWoEv = mksem("sWoEv"); sU1 = mksem("sU1")
    sLnA = mksem("sLnA"); sLnP = mksem("sLnP"); sLnR = mksem("sLnR")
    sLnB = mksem("sLnB"); sApp = mksem("sApp")
    sFz = mksem("sFz"); sFg = mksem("sFg"); sFzg = mksem("sFzg")
    sFt = mksem("sFt")
    sFW1p = [mksem("sFW1a"), mksem("sFW1b")]
    sFW2p = [mksem("sFW2a"), mksem("sFW2b"), mksem("sFW2c")]
    sFb1 = mksem("sFb1"); sFb2 = mksem("sFb2"); sFh = mksem("sFh")
    sFu = mksem("sFu"); sFu2 = mksem("sFu2")
    sCWp = [mksem("sCW0"), mksem("sCW1")]
    sCSp = [mksem("sCS0"), mksem("sCS1")]
    sXOp = [mksem("sXO0"), mksem("sXO1"), mksem("sXO2"), mksem("sXO3")]
    sOutG = mksem("sOutG")
    sWB = mksem("sWB")
    sFW1q = [mksem("sFW1q0"), mksem("sFW1q1"), mksem("sFW1q2"), mksem("sFW1q3")]
    sOut = mksem("sOut")

    eraQ = ExitStack()
    qT = eraQ.enter_context(nc.sbuf_tensor("qT", [128, NPAIR * TOK], BF16))
    masks = eraQ.enter_context(nc.sbuf_tensor("masks", [128, 16 * QT], BF16))

    def cast_load(dst_ap, src_ap):
        P.emit("gpsimd", lambda E, o=dst_ap, i=src_ap:
               sW.inc(E.dma_start(out=o, in_=i), 16))

    p1s = ExitStack()
    sb1 = lambda nm, shp, dt: p1s.enter_context(nc.sbuf_tensor(nm, shp, dt))
    ps1 = lambda nm, shp, dt: p1s.enter_context(nc.psum_tensor(nm, shp, dt))
    wq = sb1("wq", [128, NC * D], BF16)
    wk = sb1("wk", [128, NC * D], BF16)
    wv = sb1("wv", [128, NC * D], BF16)
    xbuf = sb1("xbuf", [128, 2 * NC * 2 * QT], BF16)
    kbuf = sb1("kbuf", [128, NPAIR * 2 * QT], BF16)
    vsb = sb1("vsb", [128, 2 * 4160], BF16)
    KQp = [ps1("kqp0", [128, QT], F32), ps1("kqp1", [128, QT], F32)]
    Vp = [ps1("vp0", [128, 1024], F32), ps1("vp1", [128, 1024], F32)]

    P.emit("gpsimd", lambda E: sW.inc(E.dma_start(out=thrS[:], in_=thr[:]), 16))
    P.emit("gpsimd", lambda E: sW.inc(E.dma_start(out=bocS[:], in_=boc[:]), 16))
    P.emit("gpsimd", lambda E: sW.inc(E.dma_start(out=b1cS[:], in_=b1c[:]), 16))
    P.emit("gpsimd", lambda E: sW.inc(E.dma_start(out=b2cS[:], in_=b2c[:]), 16))
    thr_ready = sW.n
    for c in range(NC):
        cast_load(wq[:, c * D:(c + 1) * D], wqT[c * 128:(c + 1) * 128, :])
    wq_ready = sW.n
    for g0 in range(2):
        for c in range(NC):
            P.emit("gpsimd", lambda E,
                   o=xbuf[:, g0 * 8192 + c * 1024: g0 * 8192 + (c + 1) * 1024],
                   i_=xTo[c * 128:(c + 1) * 128, g0 * 1024:(g0 + 1) * 1024],
                   sx=sXp[g0]:
                   sx.inc(E.dma_start(out=o, in_=i_), 16))
    for c in range(NC):
        cast_load(wk[:, c * D:(c + 1) * D], wkT[c * 128:(c + 1) * 128, :])
        cast_load(wv[:, c * D:(c + 1) * D], wvT[c * 128:(c + 1) * 128, :])
    wkv_ready = sW.n

    P.emit("gpsimd", lambda E: sMsk.inc(E.iota(
        iot[:], pattern=[[4, QT]], base=0, channel_multiplier=-1,
        allow_small_or_imprecise_dtypes=True)))
    P.wait("vector", sW, thr_ready)
    P.wait("vector", sMsk)
    for b_ in range(16):
        P.emit("vector", lambda E, b_=b_: sMsk.inc(E.tensor_scalar(
            masks[:, b_ * QT:(b_ + 1) * QT], iot[:], thrS[:, b_:b_ + 1], None,
            AluOpType.is_ge)))
    P.wait("gpsimd", sMsk)
    P.emit("gpsimd", lambda E: sMsk.inc(E.iota(
        iot[:], pattern=[[1, QT]], base=0, channel_multiplier=-1,
        allow_small_or_imprecise_dtypes=True)))
    P.wait("vector", sMsk)
    P.emit("vector", lambda E: sMsk.inc(E.tensor_scalar(
        ident[:], iot[:, 0:128], 0.0, None, AluOpType.is_equal)))
    P.emit("vector", lambda E: E.memset(sel33[:], 0.0))
    P.emit("vector", lambda E: E.drain())
    P.emit("vector", lambda E: E.memset(sel33[0:1, 0:64], 1.0))
    P.emit("vector", lambda E: E.memset(sel33[32:33, 64:128], 1.0))
    P.emit("vector", lambda E: E.memset(ones1[:], 1.0))
    P.emit("vector", lambda E: E.memset(ones128[:], 1.0))
    P.emit("vector", lambda E: E.memset(epsc[:], EPS))
    P.emit("vector", lambda E: sMsk.inc(E.memset(vsb[:], 1.0)))
    msk_done = sMsk.n

    kq_n = [0]; v_n = [0]; tile_n = [0]; kdma = [0]; vdma = [0]
    tile_end_pe = []

    def load_xgroup(src2d, base):
        g = tile_n[0] // 2
        gpar = g % 2
        if g >= 2:
            P.wait("gpsimd", sPE, tile_end_pe[2 * g - 3])
        for c in range(NC):
            P.emit("gpsimd", lambda E,
                   o=xbuf[:, gpar * 8192 + c * 1024: gpar * 8192 + (c + 1) * 1024],
                   i_=src2d[c * 128:(c + 1) * 128, base: base + 1024],
                   sx=sXp[gpar]:
                   sx.inc(E.dma_start(out=o, in_=i_), 16))

    def xcol(gpar, tp, c):
        return gpar * 8192 + c * 1024 + tp * QT

    def kq_proj(gpar, tp, wmat, dst_fn):
        for p in range(NPAIR):
            idx = kq_n[0]
            bank = KQp[idx % 2]
            if idx >= 2:
                P.wait("tensor", sEvKQ, idx - 1)
            for c in range(NC):
                lastmm = c == NC - 1
                def mm(E, o=bank[:], l=wmat[:, c * D + p * 128: c * D + (p + 1) * 128],
                       r=xbuf[:, xcol(gpar, tp, c): xcol(gpar, tp, c) + QT],
                       st=(c == 0), sp=lastmm):
                    return E.matmul(o, l, r, start=st, stop=sp)
                if lastmm:
                    P.emit("tensor", lambda E, f=mm: sPE.inc(f(E)))
                else:
                    P.emit("tensor", lambda E, f=mm: f(E))
            kq_n[0] += 1
            P.wait("vector", sPE)
            dst_fn(p, bank)

    for i in range(TOK // QT):
        gpar = (tile_n[0] // 2) % 2
        tp = tile_n[0] % 2
        P.wait("tensor", sXp[gpar])
        if i == 0:
            P.wait("tensor", sW, wq_ready)

        def qdst(p, bank, i=i):
            P.emit("vector", lambda E,
                   o=qT[:, p * TOK + i * QT: p * TOK + (i + 1) * QT],
                   i_=bank[:]: sEvKQ.inc(E.tensor_copy(o, i_)))
        kq_proj(gpar, tp, wq, qdst)
        tile_n[0] += 1
        tile_end_pe.append(sPE.n)

    for i in range(S // QT):
        gpar = (tile_n[0] // 2) % 2
        tp = tile_n[0] % 2
        if tp == 0:
            load_xgroup(xTb, (i // 2) * 1024)
        P.wait("tensor", sXp[gpar])
        if i == 0:
            P.wait("tensor", sW, wkv_ready)

        def kdst(p, bank, i=i):
            g_, tp = i // 2, i % 2
            if g_ >= 1:
                P.wait("vector", sKWp[0], 16 * (NPAIR * (g_ - 1) + p + 1))
            P.emit("vector", lambda E,
                   o=kbuf[:, p * 1024 + tp * QT: p * 1024 + tp * QT + QT],
                   i_=bank[:]: sEvKQ.inc(E.tensor_copy(o, i_)))
            kdma[0] += 1
        kq_proj(gpar, tp, wk, kdst)
        if i % 2 == 1:
            P.wait("sync", sEvKQ)
            for p in range(NPAIR):
                P.emit("sync", lambda E, p=p, i=i: sKWp[0].inc(E.dma_start(
                    out=kscr[p, :, (i - 1) * QT:(i + 1) * QT],
                    in_=kbuf[:, p * 1024:(p + 1) * 1024]), 16))

        tpar = i % 2
        for ts_ in range(4):
            vi = v_n[0]
            bank = Vp[vi % 2]
            if vi >= 2:
                P.wait("tensor", sEvV, vi - 1)
            for hf in range(2):
                for c in range(NC):
                    lastmm = (hf == 1) and (c == NC - 1)
                    def vmm(E, o=bank[:, hf * QT:(hf + 1) * QT],
                            l=xbuf[:, xcol(gpar, tp, c) + ts_ * 128:
                                   xcol(gpar, tp, c) + ts_ * 128 + 128],
                            r=wv[:, c * D + hf * QT: c * D + (hf + 1) * QT],
                            st=(c == 0), sp=(c == NC - 1)):
                        return E.matmul(o, l, r, start=st, stop=sp)
                    if lastmm:
                        P.emit("tensor", lambda E, f=vmm: sPE.inc(f(E)))
                    else:
                        P.emit("tensor", lambda E, f=vmm: f(E))
            v_n[0] += 1
            P.wait("vector", sPE)
            if vi == 0:
                P.wait("vector", sMsk, msk_done)
            if ts_ == 0 and i >= 2:
                P.wait("vector", sVWp[tpar], 16 * 8 * (i // 2))
            P.emit("vector", lambda E, b=bank, tpar=tpar, ts_=ts_:
                   sEvV.inc(E.tensor_copy(
                       vsb[:, tpar * 4160:(tpar + 1) * 4160]
                       .rearrange("p (pr s h c) -> p pr s h c",
                                  pr=NPAIR, s=4, h=2, c=65)[:, :, ts_, :, 0:64],
                       b[:].rearrange("p (pr h c) -> p pr h c",
                                      pr=NPAIR, h=2, c=64))))
        P.wait("sync", sEvV)
        for p in range(NPAIR):
            P.emit("sync", lambda E, p=p, i=i, tpar=tpar, sv=sVWp[tpar]:
                   sv.inc(E.dma_start(
                       out=vscr[p, :, (i * 4) * 130:(i * 4 + 4) * 130],
                       in_=vsb[:, tpar * 4160 + p * 520:
                               tpar * 4160 + (p + 1) * 520]), 16))
        vdma[0] += 8
        tile_n[0] += 1
        tile_end_pe.append(sPE.n)

    for c in range(NC):
        cast_load(woS[:, c * D:(c + 1) * D], woT[c * 128:(c + 1) * 128, :])
        cast_load(a1r1S[:, c * 32:(c + 1) * 32], a1r1[c * 128:(c + 1) * 128, :])
    for c in range(HC):
        cast_load(a2r2S[:, c * 32:(c + 1) * 32], a2r2[c * 128:(c + 1) * 128, :])
    cast_load(b1S[:], b1cat[:])
    cast_load(b2S[:], b2cat[:])
    w_ready = sW.n

    for hc in range(HC):
        P.emit("gpsimd", lambda E, hc=hc: sWB.inc(E.dma_start(
            out=w1bf[hc],
            in_=w1d.rearrange("(c p) h -> p c h", p=128)[:, :, hc * 128:(hc + 1) * 128]), 16))
    for dc in range(NC):
        P.emit("gpsimd", lambda E, dc=dc: sWB.inc(E.dma_start(
            out=w2bf[dc],
            in_=w2d.rearrange("(c p) d -> p c d", p=128)[:, :, dc * 128:(dc + 1) * 128]), 16))
    wb_done = sWB.n

    p1_kw = (sKWp[0].n, sKWp[1].n)
    p1_vw = (sVWp[0].n, sVWp[1].n)
    p1_evkq = sEvKQ.n
    p1_evv = sEvV.n

    if debug:
        P.wait("gpsimd", sEvKQ, p1_evkq)
        for p in range(NPAIR):
            P.emit("gpsimd", lambda E, p=p: sOutG.inc(E.dma_start(
                out=dbg["dbg_q"][p * 128:(p + 1) * 128, :],
                in_=qT[:, p * TOK:(p + 1) * TOK]), 16))
        P.wait("gpsimd", sKWp[0], p1_kw[0])
        P.wait("gpsimd", sKWp[1], p1_kw[1])
        P.wait("gpsimd", sVWp[0], p1_vw[0])
        P.wait("gpsimd", sVWp[1], p1_vw[1])
        for p in range(NPAIR):
            P.emit("gpsimd", lambda E, p=p: sOutG.inc(E.dma_start(
                out=dbg["dbg_k"][p * 128:(p + 1) * 128, :], in_=kscr[p, :, :]), 16))
            P.emit("gpsimd", lambda E, p=p: sOutG.inc(E.dma_start(
                out=dbg["dbg_v"][p * 128:(p + 1) * 128, :], in_=vscr[p, :, :]), 16))

    ckpt(2)
    p1s.close()
    fls = ExitStack()
    sbf = lambda nm, shp, dt: fls.enter_context(nc.sbuf_tensor(nm, shp, dt))
    psf = lambda nm, shp, dt: fls.enter_context(nc.psum_tensor(nm, shp, dt))
    kst = sbf("kst", [128, 2 * S], BF16)
    vst = sbf("vst", [128, 2 * NKT * 130], BF16)
    pT = sbf("pT", [128, 2 * SET_COLS], BF16)
    ctmp = sbf("ctmp", [128, 2 * QT], BF16)
    SAp = [psf("sa0", [128, SET_COLS], F32), psf("sa1", [128, SET_COLS], F32)]
    CHp = [psf("ch0", [128, QT], F32), psf("ch1", [128, QT], F32)]

    P.wait("sync", sKWp[0], p1_kw[0])
    P.wait("sync", sKWp[1], p1_kw[1])
    P.wait("sync", sVWp[0], p1_vw[0])
    P.wait("sync", sVWp[1], p1_vw[1])
    P.wait("sync", sPE, tile_end_pe[-1])
    P.wait("sync", sEvKQ, p1_evkq)
    P.wait("sync", sEvV, p1_evv)

    pair_end_av = []

    def stage_pair(p):
        par = p % 2
        if p >= 2:
            P.wait("sync", sAV, pair_end_av[p - 2])
        P.emit("sync", lambda E, p=p, par=par, ss=sStgp[par]: ss.inc(
            E.dma_start(out=kst[:, par * S:(par + 1) * S], in_=kscr[p, :, :]), 16))
        P.emit("sync", lambda E, p=p, par=par, ss=sStgp[par]: ss.inc(
            E.dma_start(out=vst[:, par * NKT * 130:(par + 1) * NKT * 130],
                        in_=vscr[p, :, :]), 16))

    stage_pair(0)
    stage_pair(1)

    P.wait("vector", sEvV)
    P.wait("vector", sEvKQ)
    P.emit("vector", lambda E: E.memset(SAp[0][:], 0.0))
    P.emit("vector", lambda E: sCev.inc(E.memset(SAp[1][:], 0.0)))

    def slot_sets(s_):
        qs = []
        nk = 16 * s_ + 16
        for kt in range(min(nk, 16 * s_)):
            for h in range(2):
                qs.append((h, kt, 0, QT))
        for h in range(2):
            for kt in range(16 * s_, min(nk, FLASH_KT_LIMIT)):
                q0 = 32 * (kt - 16 * s_)
                qs.append((h, kt, q0, QT - q0))
        sets = []
        cur, bank_i, bank_used, bank_h = [], 0, 0, None
        for (h, kt, q0, n) in qs:
            if bank_used + n > QT or (bank_h is not None and bank_h != h):
                bank_i += 1
                bank_used = 0
                bank_h = None
            if bank_i == 3:
                sets.append(cur)
                cur, bank_i, bank_used, bank_h = [], 0, 0, None
            cur.append((h, kt, q0, n, bank_i * QT + bank_used))
            bank_used += n
            bank_h = h
        if cur:
            sets.append(cur)
        return sets

    g_set = [0]
    mask_cnt = [0]
    mask_hist = {-1: 0, 0: 0}
    lw_n = [0]

    pending_av = [None]
    for p in range(min(NPAIR, FLASH_PAIRS)):
        par = p % 2
        P.wait("tensor", sStgp[p % 2], 32 * (p // 2 + 1))
        if p == 0:
            P.wait("tensor", sMsk, msk_done)
            P.wait("scalar", sMsk, msk_done)
            P.wait("tensor", sCev)
        for s_ in range(min(NSLOT, FLASH_SLOTS)):
            sets = slot_sets(s_)
            first_av = {0: True, 1: True}
            cev_target = sCev.n
            n_sets = len(sets)

            def emit_av(g, st_, buf, av_meta,
                        first_av=first_av, cev_target=cev_target,
                        par=par, s_=s_, p=p):
                if mask_hist[g] > mask_hist.get(g - 1, 0):
                    P.wait("tensor", sMa, mask_hist[g])
                else:
                    P.wait("tensor", sExp, g + 1)
                if av_meta["first"]:
                    P.wait("tensor", sCev, cev_target)
                for qi, (h, kt, q0, n, off) in enumerate(st_):
                    stt = first_av[h]
                    first_av[h] = False
                    last_q = qi == len(st_) - 1
                    stp = kt == 16 * s_ + 15
                    def avmm(E, h=h, kt=kt, q0=q0, n=n, off=off, buf=buf,
                             par=par, stt=stt, stp=stp):
                        return E.matmul(
                            CHp[h][0:65, q0:QT],
                            vst[:, par * NKT * 130 + kt * 130 + h * 65:
                                par * NKT * 130 + kt * 130 + h * 65 + 65],
                            pT[:, buf * SET_COLS + off: buf * SET_COLS + off + n],
                            start=stt, stop=stp)
                    if last_q:
                        P.emit("tensor", lambda E, f=avmm: sAV.inc(f(E)))
                    else:
                        P.emit("tensor", lambda E, f=avmm: f(E))

            for si, st_ in enumerate(sets):
                if g_set[0] >= FLASH_SETS:
                    raise _StopBuild(nc, P)
                g = g_set[0]
                buf = g % 2
                if g >= 2:
                    P.wait("tensor", sExp, g - 1)
                for qi, (h, kt, q0, n, off) in enumerate(st_):
                    last = qi == len(st_) - 1
                    def smm(E, h=h, kt=kt, q0=q0, n=n, off=off, buf=buf,
                            p=p, s_=s_, par=par):
                        return E.matmul(
                            SAp[buf][:, off:off + n],
                            kst[h * 64:(h + 1) * 64,
                                par * S + kt * 128: par * S + (kt + 1) * 128],
                            qT[h * 64:(h + 1) * 64,
                               p * TOK + s_ * QT + q0: p * TOK + (s_ + 1) * QT],
                            start=True, stop=True)
                    if last:
                        P.emit("tensor", lambda E, f=smm: sS.inc(f(E)))
                    else:
                        P.emit("tensor", lambda E, f=smm: f(E))
                if pending_av[0] is not None:
                    pending_av[0]()
                P.wait("scalar", sS, g + 1)
                if g >= 2:
                    P.wait("scalar", sAV, g - 1)
                    P.wait("scalar", sMa, mask_hist[g - 2])
                segs = {}
                for (h_, kt_, q0_, n_, off_) in st_:
                    bnk = off_ // QT
                    lo, hi = segs.get(bnk, (off_, off_))
                    segs[bnk] = (min(lo, off_), max(hi, off_ + n_))
                seg_list = []
                for lo, hi in sorted(segs.values()):
                    if seg_list and lo - seg_list[-1][1] < 600:
                        seg_list[-1] = (seg_list[-1][0], hi)
                    else:
                        seg_list.append((lo, hi))
                for gi, (lo, hi) in enumerate(seg_list):
                    last_seg = gi == len(seg_list) - 1
                    def expop(E, buf=buf, lo=lo, hi=hi):
                        return E.activation(
                            out=pT[:, buf * SET_COLS + lo: buf * SET_COLS + hi],
                            in_=SAp[buf][:, lo:hi], func=AF.Exp, scale=SMSCALE)
                    if last_seg:
                        P.emit("scalar", lambda E, f=expop: sExp.inc(f(E)))
                    else:
                        P.emit("scalar", lambda E, f=expop: f(E))
                P.wait("vector", sExp, g + 1)
                for (h, kt, q0, n, off) in st_:
                    b_ = kt - 16 * s_
                    if b_ >= 0:
                        P.emit("vector", lambda E, buf=buf, off=off, n=n, b_=b_, q0=q0:
                               sMa.inc(E.tensor_tensor(
                                   pT[:, buf * SET_COLS + off: buf * SET_COLS + off + n],
                                   pT[:, buf * SET_COLS + off: buf * SET_COLS + off + n],
                                   masks[:, b_ * QT + q0:(b_ + 1) * QT],
                                   AluOpType.mult)))
                        mask_cnt[0] += 1
                mask_hist[g] = mask_cnt[0]
                av_meta = {"first": first_av[0]}
                pending_av[0] = (lambda g=g, st_=st_, buf=buf, av_meta=av_meta,
                                 f=emit_av: f(g, st_, buf, av_meta))
                g_set[0] += 1
            if not FLASH_EVAC:
                raise _StopBuild(nc, P)
            P.wait("vector", sAV, g_set[0])
            n_ev = lw_n[0]
            cpar = n_ev % 2
            if n_ev >= 2:
                P.wait("vector", sCWp[cpar], 16 * (n_ev // 2))
                P.wait("vector", sLWp[cpar], 32 * (n_ev // 2))
            r = p * NSLOT + s_
            P.emit("vector", lambda E, cpar=cpar: sCev.inc(E.tensor_copy(
                ctmp[0:64, cpar * QT:(cpar + 1) * QT], CHp[0][0:64, :])))
            P.emit("vector", lambda E, cpar=cpar: sCev.inc(E.tensor_copy(
                ctmp[64:128, cpar * QT:(cpar + 1) * QT], CHp[1][0:64, :])))
            P.emit("vector", lambda E, cpar=cpar: sCev.inc(E.tensor_copy(
                lrow[0:1, cpar * QT:(cpar + 1) * QT], CHp[0][64:65, :])))
            P.emit("vector", lambda E, cpar=cpar: sCev.inc(E.tensor_copy(
                lrow[32:33, cpar * QT:(cpar + 1) * QT], CHp[1][64:65, :])))
            P.wait("sync", sCev)
            P.emit("sync", lambda E, p=p, s_=s_, cpar=cpar, sc=sCWp[cpar]:
                   sc.inc(E.dma_start(
                       out=ctxscr[p, :, s_ * QT:(s_ + 1) * QT],
                       in_=ctmp[:, cpar * QT:(cpar + 1) * QT]), 16))
            P.emit("sync", lambda E, r=r, cpar=cpar, sl=sLWp[cpar]: sl.inc(E.dma_start(
                out=lscr[0, r, :], in_=lrow[0:1, cpar * QT:(cpar + 1) * QT]), 16))
            P.emit("sync", lambda E, r=r, cpar=cpar, sl=sLWp[cpar]: sl.inc(E.dma_start(
                out=lscr[1, r, :], in_=lrow[32:33, cpar * QT:(cpar + 1) * QT]), 16))
            lw_n[0] += 1
        pair_end_av.append(g_set[0])
        if p + 2 < NPAIR:
            stage_pair(p + 2)

    if pending_av[0] is not None:
        pending_av[0]()
    flash_cev = sCev.n
    flash_lw = (sLWp[0].n, sLWp[1].n)
    flash_cw = (sCWp[0].n, sCWp[1].n)

    ckpt(3)
    fls.close()
    eraQ.close()

    eraU = ExitStack()
    u1 = eraU.enter_context(nc.sbuf_tensor("u1", [128, NC * TOK], F32))

    p3s = ExitStack()
    sb3 = lambda nm, shp, dt: p3s.enter_context(nc.sbuf_tensor(nm, shp, dt))
    ps3 = lambda nm, shp, dt: p3s.enter_context(nc.psum_tensor(nm, shp, dt))
    ltile = sb3("ltile", [32, QT], F32)
    ltile2 = sb3("ltile2", [32, QT], F32)
    rl33 = sb3("rl33", [33, 2 * 4096], F32)
    LBsb = sb3("LBsb", [128, 4096], F32)
    cstage = sb3("cstage", [128, 2 * NC * QT], BF16)
    xot = sb3("xot", [128, 4 * QT], F32)
    u1t = sb3("u1t", [128, 2 * QT], F32)
    LBp = ps3("LBp", [128, 2048], F32)
    WOp = [ps3("wop0", [128, QT], F32), ps3("wop1", [128, QT], F32)]

    if debug:
        P.wait("gpsimd", sCWp[0], flash_cw[0])
        P.wait("gpsimd", sCWp[1], flash_cw[1])
        P.wait("gpsimd", sLWp[0], flash_lw[0])
        P.wait("gpsimd", sLWp[1], flash_lw[1])
        for p in range(NPAIR):
            P.emit("gpsimd", lambda E, p=p: sOutG.inc(E.dma_start(
                out=dbg["dbg_ctx"][p * 128:(p + 1) * 128, :],
                in_=ctxscr[p, :, :]), 16))
        P.emit("gpsimd", lambda E: sOutG.inc(E.dma_start(
            out=dbg["dbg_l"][:], in_=lscr[:, :, :]), 16))

    P.wait("sync", sLWp[0], flash_lw[0])
    P.wait("sync", sLWp[1], flash_lw[1])
    if debug:
        P.wait("sync", sOutG)
    P.emit("sync", lambda E: sLd.inc(E.dma_start(out=ltile[:], in_=lscr[0, :, :]), 16))
    P.emit("sync", lambda E: sLd.inc(E.dma_start(out=ltile2[:], in_=lscr[1, :, :]), 16))
    P.wait("vector", sLd)
    P.emit("vector", lambda E: sRe.inc(E.reciprocal(ltile[:], ltile[:])))
    P.emit("vector", lambda E: sRe.inc(E.reciprocal(ltile2[:], ltile2[:])))
    P.emit("vector", lambda E: sRe.inc(E.memset(rl33[:], 0.0)))
    P.wait("sync", sRe)
    P.emit("sync", lambda E: sLdW.inc(E.dma_start(out=lscr[0], in_=ltile[:]), 16))
    P.emit("sync", lambda E: sLdW.inc(E.dma_start(out=lscr[1], in_=ltile2[:]), 16))
    recip_w = sLdW.n

    P.wait("sync", sLdW, recip_w)
    P.wait("sync", sCWp[0], flash_cw[0])
    P.wait("sync", sCWp[1], flash_cw[1])
    P.wait("tensor", sW, w_ready)
    P.wait("scalar", sW, w_ready)
    wo_n = [0]
    ld_n = [0]
    xo_n = [0]
    for k in range(2):
        P.emit("sync", lambda E, k=k, sx=sXOp[k]: sx.inc(E.dma_start(
            out=xot[:, k * QT:(k + 1) * QT],
            in_=xTo[k * 128:(k + 1) * 128, 0:QT]), 16))
        xo_n[0] += 1
    for tt in range(TOK // QT):
        par = tt % 2
        if tt >= 2:
            P.wait("sync", sWoPE, 8 * (tt - 1))
            P.wait("sync", sNrmM, 8 * (tt - 1))
        P.emit("sync", lambda E, tt=tt, par=par, sc=sCSp[par]:
               sc.inc(E.dma_start(
                   out=cstage.rearrange("p (b cc q) -> p b cc q",
                                        b=2, cc=NC)[:, par],
                   in_=ctxscr[:, :, tt * QT:(tt + 1) * QT]
                       .rearrange("cc p q -> p cc q")), 16))
        P.emit("sync", lambda E, tt=tt, par=par, sl=sLdp[par]: sl.inc(E.dma_start(
            out=rl33[0:1, par * 4096:(par + 1) * 4096]
                .rearrange("p (r q) -> p r q", r=NC),
            in_=lscr[0].rearrange("(c s) q -> c s q", s=NSLOT)[:, tt]), 16))
        P.emit("sync", lambda E, tt=tt, par=par, sl=sLdp[par]: sl.inc(E.dma_start(
            out=rl33[32:33, par * 4096:(par + 1) * 4096]
                .rearrange("p (r q) -> p r q", r=NC),
            in_=lscr[1].rearrange("(c s) q -> c s q", s=NSLOT)[:, tt]), 16))
        P.wait("tensor", sLdp[par])
        P.wait("tensor", sRe, 3)
        for w in range(2):
            P.wait("tensor", sNrmC, 2 * tt + w)
            for k in range(4):
                cc = w * 4 + k
                P.emit("tensor", lambda E, cc=cc, k=k, par=par: sNrmM.inc(E.matmul(
                    LBp[:, k * QT:(k + 1) * QT], sel33[:],
                    rl33[0:33, par * 4096 + cc * QT: par * 4096 + (cc + 1) * QT],
                    start=True, stop=True)))
            P.wait("vector", sNrmM)
            P.emit("vector", lambda E, w=w: sNrmC.inc(E.tensor_copy(
                LBsb[:, w * 2048:(w + 1) * 2048], LBp[:])))
        P.wait("vector", sNrmC, 2 * (tt + 1))
        P.wait("vector", sCSp[par], 16 * (tt // 2 + 1))
        P.emit("vector", lambda E, par=par: sNrmX.inc(E.tensor_tensor(
            cstage[:, par * NC * QT:(par + 1) * NC * QT],
            cstage[:, par * NC * QT:(par + 1) * NC * QT],
            LBsb[:], AluOpType.mult)))
        stage_done = sNrmX.n
        for c in range(NC):
            i = wo_n[0]
            bank = WOp[i % 2]
            nxt = i + 2
            if nxt < 32:
                ntt, ncz = nxt // NC, nxt % NC
                if xo_n[0] >= 4:
                    P.wait("sync", sU1, xo_n[0] - 3)
                P.emit("sync", lambda E, ncz=ncz, ntt=ntt, sl=(xo_n[0] % 4),
                       sx=sXOp[xo_n[0] % 4]:
                       sx.inc(E.dma_start(
                           out=xot[:, sl * QT:(sl + 1) * QT],
                           in_=xTo[ncz * 128:(ncz + 1) * 128,
                                   ntt * QT:(ntt + 1) * QT]), 16))
                xo_n[0] += 1
            if i >= 2:
                P.wait("tensor", sWoEv, i - 1)
            P.wait("tensor", sNrmX, stage_done)
            for cc in range(NC):
                lastmm = cc == NC - 1
                def wmm(E, o=bank[:], l=woS[:, cc * D + c * 128: cc * D + (c + 1) * 128],
                        r=cstage[:, (par * NC + cc) * QT:(par * NC + cc + 1) * QT],
                        st=(cc == 0), sp=lastmm):
                    return E.matmul(o, l, r, start=st, stop=sp)
                if lastmm:
                    P.emit("tensor", lambda E, f=wmm: sWoPE.inc(f(E)))
                else:
                    P.emit("tensor", lambda E, f=wmm: f(E))
            wo_n[0] += 1
            P.wait("scalar", sWoPE)
            if i >= 2:
                P.wait("scalar", sU1, i - 1)
            P.emit("scalar", lambda E, bank=bank, c=c, i=i: sWoEv.inc(E.activation(
                out=u1t[:, (i % 2) * QT:(i % 2 + 1) * QT], in_=bank[:],
                func=AF.Identity, bias=bocS[:, c:c + 1])))
            P.wait("vector", sWoEv)
            P.wait("vector", sXOp[i % 4], 16 * (i // 4 + 1))
            P.emit("vector", lambda E, c=c, tt=tt, i=i: sU1.inc(E.tensor_tensor(
                u1[:, c * TOK + tt * QT: c * TOK + (tt + 1) * QT],
                u1t[:, (i % 2) * QT:(i % 2 + 1) * QT],
                xot[:, (i % 4) * QT:(i % 4 + 1) * QT],
                AluOpType.add)))
    u1_done = sU1.n

    if debug:
        P.wait("sync", sU1, u1_done)
        for c in range(NC):
            P.emit("sync", lambda E, c=c: sOut.inc(E.dma_start(
                out=dbg["dbg_u1"][c * 128:(c + 1) * 128, :],
                in_=u1[:, c * TOK:(c + 1) * TOK]), 16))
        P.wait("vector", sOut)

    ckpt(4)
    p3s.close()

    ln_id = [0]

    def emit_ln(src, dst_bf16, out_dram, ready_sem, ready_val, dbg_out=None):
        ln_id[0] += 1
        tag = f"_{ln_id[0]}"
        ln = ExitStack()
        sbl = lambda nm, shp, dt: ln.enter_context(nc.sbuf_tensor(nm + tag, shp, dt))
        psl = lambda nm, shp, dt: ln.enter_context(nc.psum_tensor(nm + tag, shp, dt))
        xsq = sbl("xsq", [128, 2 * TOK], F32)
        rm = sbl("rm", [1, TOK], F32)
        rv = sbl("rv", [1, TOK], F32)
        ra = sbl("ra", [1, TOK], F32)
        bcA = sbl("bcA", [128, TOK], F32)
        bcB = sbl("bcB", [128, TOK], F32)
        stats = psl("stats", [128, TOK], F32)
        bcp = psl("bcp", [128, TOK], F32)
        a0 = sLnA.n
        p0_ = sLnP.n
        P.wait("tensor", ready_sem, ready_val)
        P.wait("scalar", ready_sem, ready_val)
        for c in range(NC):
            xpar = c % 2
            if c >= 2:
                P.wait("scalar", sLnP, p0_ + c - 1)
            P.emit("scalar", lambda E, c=c, xpar=xpar: sLnA.inc(E.activation(
                out=xsq[:, xpar * TOK:(xpar + 1) * TOK],
                in_=src[:, c * TOK:(c + 1) * TOK], func=AF.Square)))
            for nt in range(TOK // QT):
                P.emit("tensor", lambda E, c=c, nt=nt: E.matmul(
                    stats[0:1, nt * QT:(nt + 1) * QT], ones128[:],
                    src[:, c * TOK + nt * QT: c * TOK + (nt + 1) * QT],
                    start=(c == 0), stop=(c == NC - 1)))
            P.wait("tensor", sLnA, a0 + c + 1)
            for nt in range(TOK // QT):
                lastmm = nt == TOK // QT - 1
                def qmm(E, c=c, nt=nt, xpar=xpar):
                    return E.matmul(
                        stats[32:33, nt * QT:(nt + 1) * QT], ones128[:],
                        xsq[:, xpar * TOK + nt * QT: xpar * TOK + (nt + 1) * QT],
                        start=(c == 0), stop=(c == NC - 1))
                if lastmm:
                    P.emit("tensor", lambda E, f=qmm: sLnP.inc(f(E)))
                else:
                    P.emit("tensor", lambda E, f=qmm: f(E))
        P.wait("vector", sLnP, p0_ + NC)
        P.emit("vector", lambda E: sLnR.inc(E.tensor_scalar(
            rm[:], stats[0:1, :], 1.0 / D, None, AluOpType.mult)))
        P.emit("vector", lambda E: sLnR.inc(E.tensor_scalar(
            rv[:], stats[32:33, :], 1.0 / D, None, AluOpType.mult)))
        P.wait("vector", sLnR)
        P.emit("vector", lambda E: sLnR.inc(E.tensor_tensor(
            ra[:], rm[:], rm[:], AluOpType.mult)))
        P.wait("vector", sLnR)
        P.emit("vector", lambda E: sLnR.inc(E.tensor_tensor(
            rv[:], rv[:], ra[:], AluOpType.subtract)))
        P.wait("scalar", sLnR)
        P.emit("scalar", lambda E: sLnR.inc(E.activation(
            out=ra[:], in_=rv[:], func=AF.Ln, bias=epsc[0:1, 0:1])))
        P.wait("scalar", sLnR)
        P.emit("scalar", lambda E: sLnR.inc(E.activation(
            out=ra[:], in_=ra[:], func=AF.Exp, scale=-0.5)))
        P.wait("vector", sLnR)
        P.emit("vector", lambda E: sLnR.inc(E.tensor_tensor(
            rm[:], rm[:], ra[:], AluOpType.mult)))
        P.wait("tensor", sLnR)
        for nt in range(TOK // QT):
            last = nt == TOK // QT - 1
            def bmm1(E, nt=nt):
                return E.matmul(bcp[:, nt * QT:(nt + 1) * QT], ones1[:],
                                ra[0:1, nt * QT:(nt + 1) * QT], start=True, stop=True)
            if last:
                P.emit("tensor", lambda E, f=bmm1: sLnB.inc(f(E)))
            else:
                P.emit("tensor", lambda E, f=bmm1: f(E))
        P.wait("vector", sLnB)
        P.emit("vector", lambda E: sLnB.inc(E.tensor_copy(bcA[:], bcp[:])))
        P.wait("tensor", sLnB)
        for nt in range(TOK // QT):
            last = nt == TOK // QT - 1
            def bmm2(E, nt=nt):
                return E.matmul(bcp[:, nt * QT:(nt + 1) * QT], ones1[:],
                                rm[0:1, nt * QT:(nt + 1) * QT], start=True, stop=True)
            if last:
                P.emit("tensor", lambda E, f=bmm2: sLnB.inc(f(E)))
            else:
                P.emit("tensor", lambda E, f=bmm2: f(E))
        P.wait("vector", sLnB)
        P.emit("vector", lambda E: sLnB.inc(E.tensor_copy(bcB[:], bcp[:])))
        P.wait("vector", sLnB)
        for c in range(NC):
            P.emit("vector", lambda E, c=c: sApp.inc(E.tensor_tensor(
                src[:, c * TOK:(c + 1) * TOK], src[:, c * TOK:(c + 1) * TOK],
                bcA[:], AluOpType.mult)))
            P.wait("vector", sApp)
            P.emit("vector", lambda E, c=c: sApp.inc(E.tensor_tensor(
                src[:, c * TOK:(c + 1) * TOK], src[:, c * TOK:(c + 1) * TOK],
                bcB[:], AluOpType.subtract)))
            P.wait("vector", sApp)
            if dst_bf16 is not None:
                P.emit("vector", lambda E, c=c: sApp.inc(E.tensor_copy(
                    dst_bf16[:, c * TOK:(c + 1) * TOK],
                    src[:, c * TOK:(c + 1) * TOK])))
            if out_dram is not None:
                P.wait("sync", sApp)
                P.emit("sync", lambda E, c=c: sOut.inc(E.dma_start(
                    out=out_dram[c * 128:(c + 1) * 128, :],
                    in_=src[:, c * TOK:(c + 1) * TOK]), 16))
            if dbg_out is not None:
                P.wait("sync", sApp)
                P.emit("sync", lambda E, c=c: sOut.inc(E.dma_start(
                    out=dbg_out[c * 128:(c + 1) * 128, :],
                    in_=src[:, c * TOK:(c + 1) * TOK]), 16))
        ln.close()
        return sApp.n

    eraY = ExitStack()
    y1bf = eraY.enter_context(nc.sbuf_tensor("y1bf", [128, NC * TOK], BF16))

    eraW = ExitStack()
    w1s = eraW.enter_context(nc.sbuf_tensor("w1s", [128, 4 * 1024], BF16))
    w1_ld = [0]
    w2_ld = [0]

    def load_w1(j):
        P.emit("sync", lambda E, hc=(j % HC), sl=(j % 4), sw=sFW1q[j % 4]:
               sw.inc(E.dma_start(
                   out=w1s[:, sl * 1024:(sl + 1) * 1024], in_=w1bf[hc]), 16))
        w1_ld[0] += 1

    P.wait("sync", sWB, wb_done)
    load_w1(0)
    load_w1(1)
    load_w1(2)

    ln1_done = emit_ln(u1, y1bf, None, sU1, u1_done,
                       dbg_out=dbg.get("dbg_y1") if debug else None)
    if debug:
        P.wait("vector", sOut)
    y1 = u1

    ckpt(5)
    ffs = ExitStack()
    sbF = lambda nm, shp, dt: ffs.enter_context(nc.sbuf_tensor(nm, shp, dt))
    psF = lambda nm, shp, dt: ffs.enter_context(nc.psum_tensor(nm, shp, dt))
    hblk = sbF("hblk", [128, HC * QT], BF16)
    w2s = sbF("w2s", [128, 3 * 4096], BF16)
    zgbf = sbF("zgbf", [128, 4 * ER], BF16)
    zgT = sbF("zgT", [24, QT], BF16)
    zg2bf = sbF("zg2bf", [128, 4 * ER], BF16)
    zg2T = sbF("zg2T", [24, QT], BF16)
    ge = sbF("ge", [128, 12], F32)
    gn = sbF("gn", [128, 8], F32)
    gneg = sbF("gneg", [128, 4], F32)
    u2t = sbF("u2t", [128, 2 * QT], F32)
    Zp = [psF("Za", [128, QT], F32), psF("Zb", [128, QT], F32)]
    TPp = psF("TPp", [ER, 128], BF16)
    B1p = [psF("b1p0", [128, QT], F32), psF("b1p1", [128, QT], F32)]
    M2p = [psF("m2p0", [128, QT], F32), psF("m2p1", [128, QT], F32)]

    b1_n = [0]
    b2_n = [0]

    def load_w2(j):
        P.emit("sync", lambda E, dc=(j % NC), par=(j % 3), sw=sFW2p[j % 3]:
               sw.inc(E.dma_start(
                   out=w2s[:, par * 4096:(par + 1) * 4096], in_=w2bf[dc]), 16))
        w2_ld[0] += 1

    def gates_all(Z, dst_zg, zsem_val):
        P.wait("vector", sFz, zsem_val)
        for s in range(4):
            P.emit("vector", lambda E, s=s: sFg.inc(E.reduce_max(
                gneg[:, s:s + 1], Z[:, s * 32 + 24: s * 32 + 27],
                AX.X, negate=True)))
        P.wait("scalar", sFg)
        for s in range(4):
            P.emit("scalar", lambda E, s=s: sFg.inc(E.activation(
                out=ge[:, s * 3: s * 3 + 3], in_=Z[:, s * 32 + 24: s * 32 + 27],
                func=AF.Exp, bias=gneg[:, s:s + 1])))
        P.wait("vector", sFg)
        for s in range(4):
            P.emit("vector", lambda E, s=s: sFg.inc(E.reduce_sum(
                gn[:, s:s + 1], ge[:, s * 3: s * 3 + 3], AX.X)))
        P.wait("vector", sFg)
        P.emit("vector", lambda E: sFg.inc(E.reciprocal(gn[:, 4:8], gn[:, 0:4])))
        P.wait("vector", sFg)
        for s in range(4):
            P.emit("vector", lambda E, s=s: sFg.inc(E.tensor_scalar(
                ge[:, s * 3: s * 3 + 3], ge[:, s * 3: s * 3 + 3],
                gn[:, 4 + s: 5 + s], None, AluOpType.mult)))
        P.wait("vector", sFg)
        for s in range(4):
            for e_ in range(3):
                last = (s == 3 and e_ == 2)
                def zgop(E, s=s, e_=e_):
                    return E.tensor_scalar(
                        dst_zg[:, s * ER + e_ * 8: s * ER + e_ * 8 + 8],
                        Z[:, s * 32 + e_ * 8: s * 32 + e_ * 8 + 8],
                        ge[:, s * 3 + e_: s * 3 + e_ + 1], LSCALE,
                        AluOpType.mult, AluOpType.mult)
                if last:
                    P.emit("vector", lambda E, f=zgop: sFzg.inc(f(E)))
                else:
                    P.emit("vector", lambda E, f=zgop: f(E))

    P.wait("sync", sApp, ln1_done)
    P.wait("tensor", sApp, ln1_done)

    zfree = [0, 0]
    zc = [0]
    ft_z1 = [0]
    ft_z2 = [0]

    def z1_scores(tb):
        bank = Zp[zc[0] % 2]
        P.wait("tensor", sFzg, zfree[zc[0] % 2])
        for sub in range(4):
            for c in range(NC):
                lastmm = c == NC - 1
                def zmm(E, c=c, sub=sub, tb=tb, Z=bank):
                    return E.matmul(
                        Z[:, sub * 32: sub * 32 + 28],
                        y1bf[:, c * TOK + tb * QT + sub * 128:
                             c * TOK + tb * QT + (sub + 1) * 128],
                        a1r1S[:, c * 32: c * 32 + 28],
                        start=(c == 0), stop=(c == NC - 1))
                if lastmm:
                    P.emit("tensor", lambda E, f=zmm: sFz.inc(f(E)))
                else:
                    P.emit("tensor", lambda E, f=zmm: f(E))
        gates_all(bank, zgbf, sFz.n)
        zfree[zc[0] % 2] = sFzg.n
        zc[0] += 1

    def zg_transposes(src_zg, dst_zgT):
        P.wait("tensor", sFzg)
        for sub in range(4):
            if sub > 0:
                P.wait("tensor", sFt, sFt.n)
            P.emit("tensor", lambda E, sub=sub, src=src_zg: sFt.inc(E.transpose(
                TPp[:], src[:, sub * ER:(sub + 1) * ER], ident[:])))
            P.wait("vector", sFt)
            P.emit("vector", lambda E, sub=sub, dst=dst_zgT: sFt.inc(
                E.tensor_copy(dst[:, sub * 128:(sub + 1) * 128], TPp[:])))
        return sFt.n

    z1_scores(0)
    ft_z1[0] = zg_transposes(zgbf, zgT)
    for tb in range(TOK // QT):
        for j in (8 * tb, 8 * tb + 1):
            P.wait("sync", sFb2, max(0, j - 2))
            load_w2(j)
        for hc in range(HC):
            i = b1_n[0]
            par = i % 4
            if i + 3 < 4 * HC:
                P.wait("sync", sFb1, i)
                load_w1(i + 3)
            bank = B1p[i % 2]
            if i >= 2:
                P.wait("tensor", sFh, i - 1)
            P.wait("tensor", sFW1q[par], 16 * (i // 4 + 1))
            P.wait("tensor", sFt, ft_z1[0])
            for c in range(NC):
                P.emit("tensor", lambda E, c=c, par=par, bank=bank, tb=tb:
                       E.matmul(bank[:],
                                w1s[:, par * 1024 + c * 128: par * 1024 + (c + 1) * 128],
                                y1bf[:, c * TOK + tb * QT: c * TOK + (tb + 1) * QT],
                                start=(c == 0), stop=False))
            P.emit("tensor", lambda E, hc=hc, bank=bank: sFb1.inc(E.matmul(
                bank[:], b1S[:, hc * 128:(hc + 1) * 128], zgT[:],
                start=False, stop=True)))
            b1_n[0] += 1
            P.wait("scalar", sFb1)
            P.emit("scalar", lambda E, hc=hc, bank=bank: sFh.inc(E.activation(
                out=hblk[:, hc * QT:(hc + 1) * QT], in_=bank[:],
                func=AF.Relu, bias=b1cS[:, hc:hc + 1])))
        h_done = sFh.n

        if debug and tb == 0:
            P.wait("gpsimd", sFh, h_done)
            for hc in range(HC):
                P.emit("gpsimd", lambda E, hc=hc: sOutG.inc(E.dma_start(
                    out=dbg["dbg_h"][hc * 128:(hc + 1) * 128, :],
                    in_=hblk[:, hc * QT:(hc + 1) * QT]), 16))
            P.wait("scalar", sOutG)

        P.wait("tensor", sFh, h_done)
        bank_z = Zp[zc[0] % 2]
        P.wait("tensor", sFzg, zfree[zc[0] % 2])
        for sub in range(4):
            for hc in range(HC):
                lastmm = hc == HC - 1
                def z2mm(E, hc=hc, sub=sub, Z=bank_z):
                    return E.matmul(
                        Z[:, sub * 32: sub * 32 + 28],
                        hblk[:, hc * QT + sub * 128: hc * QT + (sub + 1) * 128],
                        a2r2S[:, hc * 32: hc * 32 + 28],
                        start=(hc == 0), stop=(hc == HC - 1))
                if lastmm:
                    P.emit("tensor", lambda E, f=z2mm: sFz.inc(f(E)))
                else:
                    P.emit("tensor", lambda E, f=z2mm: f(E))
        gates_all(bank_z, zg2bf, sFz.n)
        zfree[zc[0] % 2] = sFzg.n
        zc[0] += 1
        ft_z2[0] = zg_transposes(zg2bf, zg2T)
        if tb + 1 < TOK // QT:
            z1_scores(tb + 1)
        for dc in range(NC):
            i = b2_n[0]
            par = i % 3
            if i % 8 <= 5:
                P.wait("sync", sFb2, i)
                load_w2(i + 2)
            bank = M2p[i % 2]
            if i >= 2:
                P.wait("tensor", sFu, i - 1)
            P.wait("tensor", sFW2p[par], 16 * (i // 3 + 1))
            P.wait("tensor", sFt, ft_z2[0])
            for hc in range(HC):
                P.emit("tensor", lambda E, hc=hc, par=par, bank=bank:
                       E.matmul(bank[:],
                                w2s[:, par * 4096 + hc * 128: par * 4096 + (hc + 1) * 128],
                                hblk[:, hc * QT:(hc + 1) * QT],
                                start=(hc == 0), stop=False))
            P.emit("tensor", lambda E, dc=dc, bank=bank: sFb2.inc(E.matmul(
                bank[:], b2S[:, dc * 128:(dc + 1) * 128], zg2T[:],
                start=False, stop=True)))
            b2_n[0] += 1
            P.wait("scalar", sFb2)
            if i >= 2:
                P.wait("scalar", sFu2, i - 1)
            P.emit("scalar", lambda E, dc=dc, bank=bank, i=i: sFu.inc(E.activation(
                out=u2t[:, (i % 2) * QT:(i % 2 + 1) * QT], in_=bank[:],
                func=AF.Identity, bias=b2cS[:, dc:dc + 1])))
            P.wait("vector", sFu)
            P.emit("vector", lambda E, dc=dc, tb=tb, i=i: sFu2.inc(E.tensor_tensor(
                y1[:, dc * TOK + tb * QT: dc * TOK + (tb + 1) * QT],
                y1[:, dc * TOK + tb * QT: dc * TOK + (tb + 1) * QT],
                u2t[:, (i % 2) * QT:(i % 2 + 1) * QT], AluOpType.add)))
        if tb + 1 < TOK // QT:
            ft_z1[0] = zg_transposes(zgbf, zgT)

    ffn_done = sFu2.n

    if debug:
        P.wait("sync", sFu2, ffn_done)
        for c in range(NC):
            P.emit("sync", lambda E, c=c: sOut.inc(E.dma_start(
                out=dbg["dbg_u2"][c * 128:(c + 1) * 128, :],
                in_=y1[:, c * TOK:(c + 1) * TOK]), 16))
        P.wait("vector", sOut)

    ckpt(6)
    ffs.close()
    eraW.close()
    eraY.close()
    emit_ln(y1, None, yout, sFu2, ffn_done)
    eraU.close()

    P.wait("sync", sOut, sOut.n)
    if debug:
        P.wait("gpsimd", sOutG, sOutG.n)
    ckpt(99)

    _emit_block(nc, P)
    return nc


def _emit_block(nc, P):
    with nc.Block() as blk:
        blk.tensor(lambda E: [f(E) for f in P.ops["tensor"]])
        blk.vector(lambda E: [f(E) for f in P.ops["vector"]])
        blk.scalar(lambda E: [f(E) for f in P.ops["scalar"]])
        blk.gpsimd(lambda E: [f(E) for f in P.ops["gpsimd"]])
        blk.sync(lambda E: [f(E) for f in P.ops["sync"]])


_NC_CACHE = {}


def _get_nc(debug=False):
    key = bool(debug)
    if key not in _NC_CACHE:
        _NC_CACHE[key] = build_nc(debug=key)
    return _NC_CACHE[key]


def _stage(inputs, debug=False):
    x = np.asarray(inputs["L0_attn_x"], dtype=np.float32)
    Wq = np.asarray(inputs["L0_attn_Wq"], np.float32)
    Wk = np.asarray(inputs["L0_attn_Wk"], np.float32)
    Wv = np.asarray(inputs["L0_attn_Wv"], np.float32)
    Wo = np.asarray(inputs["L0_attn_Wo"], np.float32)
    bo = np.asarray(inputs["L0_attn_bo"], np.float32)
    W1 = np.asarray(inputs["L0_ffn_W1"], np.float32)
    b1 = np.asarray(inputs["L0_ffn_b1"], np.float32)
    A1 = np.asarray(inputs["L0_ffn_A1"], np.float32)
    B1 = np.asarray(inputs["L0_ffn_B1"], np.float32)
    R1 = np.asarray(inputs["L0_ffn_R1"], np.float32)
    W2 = np.asarray(inputs["L0_ffn_W2"], np.float32)
    b2 = np.asarray(inputs["L0_ffn_b2"], np.float32)
    A2 = np.asarray(inputs["L0_ffn_A2"], np.float32)
    B2 = np.asarray(inputs["L0_ffn_B2"], np.float32)
    R2 = np.asarray(inputs["L0_ffn_R2"], np.float32)

    C = np.ascontiguousarray
    shared = {
        "wqT": C(Wq.T), "wkT": C(Wk.T), "wvT": C(Wv.T), "woT": C(Wo.T),
        "boc": C(bo.reshape(NC, 128).T),
        "w1": C(W1), "b1c": C(b1.reshape(HC, 128).T),
        "w2": C(W2), "b2c": C(b2.reshape(NC, 128).T),
        "a1r1": C(np.concatenate(
            [A1.transpose(1, 0, 2).reshape(D, ER), R1,
             np.zeros((D, 5), np.float32)], axis=1)),
        "b1cat": C(B1.reshape(ER, H)),
        "a2r2": C(np.concatenate(
            [A2.transpose(1, 0, 2).reshape(H, ER), R2,
             np.zeros((H, 5), np.float32)], axis=1)),
        "b2cat": C(B2.reshape(ER, D)),
    }
    xT = [C(x[b].T) for b in range(2)]
    in_maps = []
    for core in range(8):
        b, j = core // 4, core % 4
        m = dict(shared)
        m["xTb"] = xT[b]
        m["xTo"] = C(x[b, j::4, :].T)
        t = (128.0 * np.arange(16, dtype=np.float32) - j)[None, :]
        m["thr"] = C(np.broadcast_to(t, (128, 16)))
        in_maps.append(m)
    return in_maps


def kernel(**inputs):
    nc = _get_nc(debug=DEBUG)
    in_maps = _stage(inputs, debug=DEBUG)
    res = run_bass_kernel_spmd(nc, in_maps, core_ids=list(range(8)))
    x = inputs["L0_attn_x"]
    out = np.empty((2, S, D), dtype=np.float32)
    for core in range(8):
        b, j = core // 4, core % 4
        out[b, j::4, :] = res.results[core]["y"].T
    return out
